# version 04
# baseline (speedup 1.0000x reference)
import numpy as np
import concourse.bass as bass
import concourse.mybir as mybir
from concourse.bass_utils import run_bass_kernel_spmd

F32 = mybir.dt.float32
BF16 = mybir.dt.bfloat16
AF = mybir.ActivationFunctionType
ALU = mybir.AluOpType
AX = mybir.AxisListType.X

D, E, H, C, DH, DC = 1024, 8, 16, 128, 64, 8


def build_program(T=2048, P=6144, F=4096, LB=512):
    assert T % LB == 0 and P % LB == 0 and LB % C == 0 and F % 512 == 0
    NBP, NBM = P // LB, T // LB
    CPB = LB // C
    NCHT = T // C
    TT = T // 2
    NTC = TT // 128
    FB = F // 2
    NF = FB // 128
    W1S = max(NF // 2, 1)
    NS1 = NF // W1S
    NTH = max(TT // 512, 1)
    H512 = TT // NTH

    nc = bass.Bass("TRN2", target_bir_lowering=False)
    xTm = nc.dram_tensor("xTm", [D, T + 1], F32, kind="ExternalInput").ap()
    xTp = nc.dram_tensor("xTp", [D, P + 1], F32, kind="ExternalInput").ap()
    x_tm = nc.dram_tensor("x_tm", [T, D], F32, kind="ExternalInput").ap()
    WwT = nc.dram_tensor("WwT", [D, D], F32, kind="ExternalInput").ap()
    WrT = nc.dram_tensor("WrT", [D, D], F32, kind="ExternalInput").ap()
    gateT = nc.dram_tensor("gateT", [D, E], F32, kind="ExternalInput").ap()
    dcol = nc.dram_tensor("dcol", [128, H], F32, kind="ExternalInput").ap()
    lacol = nc.dram_tensor("lacol", [128, H], F32, kind="ExternalInput").ap()
    w1t = nc.dram_tensor("w1t", [E, D, F], F32, kind="ExternalInput").ap()
    w2t = nc.dram_tensor("w2t", [E, F, D], F32, kind="ExternalInput").ap()
    b1r = nc.dram_tensor("b1r", [128, E, F // 128], F32, kind="ExternalInput").ap()
    b2 = nc.dram_tensor("b2", [E, D], F32, kind="ExternalInput").ap()
    out = nc.dram_tensor("out", [T, D], F32, kind="ExternalOutput").ap()
    xres_hbm = nc.dram_tensor("xres_hbm", [T, D], F32, kind="Internal").ap()

    def sb(name, shape, dt):
        g = nc.sbuf_tensor(name, shape, dt)
        return g, g.__enter__()

    def sbp(name, shape, dt):
        return sb(name, shape, dt)[1]

    identb = sbp("identb", [128, 128], BF16)
    w_tm = sbp("w_tm", [128, 2 * NTC, E], F32)
    w_bfT = sbp("w_bfT", [128, E], BF16)
    w_fm = sbp("w_fm", [E, 2 * NTC, 128], BF16)
    m0 = sbp("m0", [128, 1], F32)
    m1c = sbp("m1c", [128, 1], F32)
    m2c = sbp("m2c", [128, 1], F32)
    sm_t = sbp("sm_t", [128, E], F32)
    b1_sb = sbp("b1_sb", [128, E, F // 128], F32)
    b2_sb = sbp("b2_sb", [E, D], BF16)
    gateT_sb = sbp("gateT_sb", [128, DC, E], BF16)
    xres_t = [sbp(f"xres_t{i}", [128, D], F32) for i in range(2)]
    xl2tm = [sbp(f"xl2tm{i}", [128, D], BF16) for i in range(2)]
    sq2 = sbp("sq2", [128, D], BF16)
    outb = [sbp(f"outb{i}", [128, D], F32) for i in range(2)]

    gA = []

    def sba(name, shape, dt):
        g, t = sb(name, shape, dt)
        gA.append(g)
        return t

    causal = sba("causal", [128, 128], F32)
    iotaD = sba("iotaD", [128, 128], F32)
    iotaC = sba("iotaC", [128, 128], F32)
    iota_rev = sba("iota_rev", [128, 1], F32)
    ones_bf = sba("ones_bf", [128, 1], BF16)
    onesb_row = sba("onesb_row", [1, 128], BF16)
    dcol_sb = sba("dcol_sb", [128, H], F32)
    lacol_sb = sba("lacol_sb", [128, H], F32)
    lg_cols = sba("lg_cols", [128, H], F32)
    al_cols = sba("al_cols", [128, H], F32)
    al_pair = sba("al_pair", [128, DC], F32)
    lgsel = sba("lgsel", [128, DC], F32)
    lgfull = sba("lgfull", [128, H, DH], F32)
    gCst = sba("gCst", [128, DC, DH], F32)
    gwfull = sba("gwfull", [128, H, DH], BF16)
    MT = sba("MT", [128, H, 128], BF16)
    gp_all = sba("gp_all", [128, DC, 128], BF16)
    gw_tmp = sba("gw_tmp", [128, H, DH], F32)
    gp_tmp = sba("gp_tmp", [128, DC, 128], F32)
    stateT = sba("stateT", [128, DC, DH], F32)
    stateT_bf = sba("stateT_bf", [128, H, DH], BF16)
    v_tm = sba("v_tm", [128, DC, 128], BF16)
    vgw = sba("vgw", [128, DC, 128], BF16)
    wk_tm = sba("wk_tm", [128, DC, 128], BF16)
    stil = [sba(f"stil{g}", [128, 4, 128], BF16) for g in range(4)]
    rkgp = sba("rkgp", [128, DC, 128], BF16)
    arow = sba("arow", [1, LB], BF16)
    brow = sba("brow", [1, LB], BF16)
    a_sb = sba("a_sb", [128, LB], BF16)
    b_sb = sba("b_sb", [128, LB], BF16)
    sq_sb = sba("sq_sb", [128, DC, LB], BF16)
    r0 = sba("r0", [1, LB], F32)
    r1 = sba("r1", [1, LB], F32)
    r2 = sba("r2", [1, LB], F32)
    xln_m = sba("xln_m", [128, DC, T + 8], BF16)
    reads_fm = sba("reads_fm", [128, DC, T], BF16)
    WwT_sb = sba("WwT_sb", [128, DC, D], BF16)
    WrT_sb = sba("WrT_sb", [128, DC, D], BF16)
    xp_sl = sba("xp_sl", [128, 2, DC, LB + 1], BF16)
    mt_tmp = sba("mt_tmp", [128, H // 2, 128], F32)
    for g in reversed(gA):
        g.__exit__(None, None, None)
    gB = []
    gg, xln2_fm = sb("xln2_fm", [128, DC, T], BF16); gB.append(gg)
    gg, g_sb = sb("g_sb", [128, NF, TT], BF16); gB.append(gg)
    gg, w1b = sb("w1b", [128, 2, DC, W1S * 128], BF16); gB.append(gg)
    gg, w2b = sb("w2b", [128, 2, NF, 512], BF16); gB.append(gg)
    gg, y_acc = sb("y_acc", [128, NTC, D], F32); gB.append(gg)
    for g in reversed(gB):
        g.__exit__(None, None, None)

    def psm(name, shape, dt):
        g = nc.psum_tensor(name, shape, dt)
        return g, g.__enter__()

    gPA = []
    gg, v_ps = psm("v_ps", [128, 1024], F32); gPA.append(gg)
    gg, wk_ps = psm("wk_ps", [128, 1024], BF16); gPA.append(gg)
    gg, delta_ps = psm("delta_ps", [128, 512], F32); gPA.append(gg)
    gLN = []
    gg, stx_ps = psm("stx_ps", [1, 512], F32); gLN.append(gg)
    gg, sts_ps = psm("sts_ps", [1, 512], F32); gLN.append(gg)
    gg, bca_ps = psm("bca_ps", [128, 512], F32); gLN.append(gg)
    gg, bcb_ps = psm("bcb_ps", [128, 512], F32); gLN.append(gg)
    for g in reversed(gLN):
        g.__exit__(None, None, None)
    gMS = []
    gg, st_pse = psm("st_pse", [128, 512], F32); gMS.append(gg)
    gg, st_pso = psm("st_pso", [128, 512], F32); gMS.append(gg)
    gg, reads_ps = psm("reads_ps", [128, 1024], F32); gMS.append(gg)
    for g in reversed(gMS):
        g.__exit__(None, None, None)
    for g in reversed(gPA):
        g.__exit__(None, None, None)
    gP3 = []
    gg, op_ps = psm("op_ps", [128, 2, 1024], F32); gP3.append(gg)
    gg, tr_ps = psm("tr_ps", [128, 2, 1024], BF16); gP3.append(gg)
    for g in reversed(gP3):
        g.__exit__(None, None, None)
    gP4 = []
    gg, h_ps = psm("h_ps", [128, 2, 512], F32); gP4.append(gg)
    gg, y_ps = psm("y_ps", [128, 2, 512], F32); gP4.append(gg)
    gg, rt_ps = psm("rt_ps", [128, 512], F32); gP4.append(gg)
    gg, wf_ps = psm("wf_ps", [8, 2 * NTC * 128], BF16); gP4.append(gg)
    for g in reversed(gP4):
        g.__exit__(None, None, None)

    sems = {}
    for n in ["TS", "VS", "SS", "GC", "SDP", "GDP", "SR",
              "GX0", "GX1", "SX0", "SX1", "SO0", "SO1",
              "GW10", "GW11", "GW20", "GW21"]:
        g = nc.semaphore(n)
        sems[n] = g.__enter__()
    blk = nc.Block()
    block = blk.__enter__()

    def stream(which, eng):
        cnt = {k: 0 for k in sems}

        def pub(name, n=1):
            cnt[name] += n
            return cnt[name]

        def wait(name, val):
            if eng is not None and val > 0:
                eng.wait_ge(sems[name], val)

        def me(o):
            return which == o

        if me('gpsimd'):
            eng.memset(ones_bf[:], 1.0)
            eng.memset(onesb_row[:], 1.0)
            eng.iota(iotaD[:], pattern=[[1, 128]], base=-1, channel_multiplier=-1,
                     allow_small_or_imprecise_dtypes=True)
            eng.drain()
            eng.tensor_scalar_max(iotaD[:], iotaD[:], 0.0)
            eng.iota(iotaC[:], pattern=[[1, 128]], base=0, channel_multiplier=0,
                     allow_small_or_imprecise_dtypes=True)
            eng.iota(iota_rev[:], pattern=[[0, 1]], base=C - 1, channel_multiplier=-1,
                     allow_small_or_imprecise_dtypes=True)
            eng.memset(causal[:], 0.0)
            eng.memset(identb[:], 0.0)
            eng.memset(xln_m[:, :, T + 1:T + 8], 0.0)
            eng.drain()
            eng.affine_select(out=causal[:], in_=causal[:], compare_op=ALU.is_ge,
                              fill=1.0, base=0, pattern=[[-1, 128]], channel_multiplier=1)
            eng.affine_select(out=identb[:], in_=identb[:], compare_op=ALU.not_equal,
                              fill=1.0, base=0, pattern=[[-1, 128]], channel_multiplier=1)
            eng.drain().then_inc(sems['GC'], 1)
        pub('GC')
        GC_c = cnt['GC']

        if me('gpsimd'):
            eng.dma_start(out=xln_m[:, :, 0:T + 1],
                          in_=xTm.rearrange("(a p) t -> p a t", p=128)).then_inc(sems['GDP'], 16)
            eng.dma_start(out=WwT_sb[:],
                          in_=WwT.rearrange("(a p) i -> p a i", p=128)).then_inc(sems['GDP'], 16)
            eng.dma_start(out=WrT_sb[:],
                          in_=WrT.rearrange("(a p) i -> p a i", p=128)).then_inc(sems['GDP'], 16)
            eng.dma_start(out=gateT_sb[:],
                          in_=gateT.rearrange("(a p) e -> p a e", p=128)).then_inc(sems['GDP'], 16)
            eng.dma_start(out=b2_sb[:], in_=b2).then_inc(sems['GDP'], 16)
        pub('GDP', 16 * 5)
        GD_pre = cnt['GDP']

        if me('sync'):
            eng.dma_start(out=dcol_sb[:], in_=dcol).then_inc(sems['SDP'], 16)
            eng.dma_start(out=lacol_sb[:], in_=lacol).then_inc(sems['SDP'], 16)
            eng.dma_start(out=b1_sb[:], in_=b1r).then_inc(sems['SDP'], 16)
        pub('SDP', 16 * 3)
        SD_pre = cnt['SDP']

        if me('scalar'):
            wait('SDP', SD_pre)
            eng.activation(lg_cols[:], dcol_sb[:], AF.Sigmoid)
            eng.drain()
            eng.activation(lg_cols[:], lg_cols[:], AF.Ln)
            eng.activation(al_cols[:], lacol_sb[:], AF.Exp)
            eng.drain().then_inc(sems['SS'], 1)
        pub('SS')
        SS_rows = cnt['SS']

        if me('vector'):
            wait('SS', SS_rows)
            wait('GC', GC_c)
            eng.tensor_copy(lgfull[:], lg_cols.ap().unsqueeze(2).broadcast_to((128, H, DH)))
            eng.tensor_copy(al_pair[0:64, :], al_cols[0:64, 0:H:2])
            eng.tensor_copy(al_pair[64:128, :], al_cols[64:128, 1:H:2])
            eng.tensor_copy(lgsel[0:64, :], lg_cols[0:64, 0:H:2])
            eng.tensor_copy(lgsel[64:128, :], lg_cols[64:128, 1:H:2])
            eng.drain()
            for h in range(H // 2):
                eng.tensor_scalar(mt_tmp[:, h, :], iotaD[:], lg_cols[:, h:h + 1], None, ALU.mult)
            eng.tensor_tensor(gp_tmp[:], iotaC.ap().unsqueeze(1).broadcast_to((128, DC, 128)),
                              lgsel.ap().unsqueeze(2).broadcast_to((128, DC, 128)), ALU.mult)
            eng.tensor_scalar(gw_tmp[:], lgfull[:], iota_rev[:, 0:1], None, ALU.mult)
            eng.drain().then_inc(sems['VS'], 1)
            eng.memset(stateT[:], 0.0)
            eng.memset(stateT_bf[:], 0.0)
            eng.drain().then_inc(sems['VS'], 1)
        pub('VS'); VS_pre1 = cnt['VS']
        pub('VS'); VS_state0 = cnt['VS']

        if me('scalar'):
            wait('VS', VS_pre1)
            eng.activation(mt_tmp[:], mt_tmp[:], AF.Exp)
            eng.activation(gp_all.ap(), gp_tmp[:], AF.Exp)
            eng.activation(gwfull.ap(), gw_tmp[:], AF.Exp)
            eng.activation(gCst[0:64, :, :], lgfull[0:64, 0:H:2, :], AF.Exp, scale=float(C))
            eng.activation(gCst[64:128, :, :], lgfull[64:128, 1:H:2, :], AF.Exp, scale=float(C))
            eng.drain().then_inc(sems['SS'], 1)
        pub('SS'); SS_exps = cnt['SS']

        if me('vector'):
            wait('SS', SS_exps)
            eng.tensor_tensor(MT[:, 0:H // 2, :], mt_tmp[:],
                              causal.ap().unsqueeze(1).broadcast_to((128, H // 2, 128)), ALU.mult)
            eng.drain()
            for h in range(H // 2, H):
                eng.tensor_scalar(mt_tmp[:, h - H // 2, :], iotaD[:],
                                  lg_cols[:, h:h + 1], None, ALU.mult)
            eng.drain().then_inc(sems['VS'], 1)
        pub('VS'); VS_mt2 = cnt['VS']
        if me('scalar'):
            wait('VS', VS_mt2)
            eng.activation(mt_tmp[:], mt_tmp[:], AF.Exp)
            eng.drain().then_inc(sems['SS'], 1)
        pub('SS'); SS_mt2 = cnt['SS']
        if me('vector'):
            wait('SS', SS_mt2)
            eng.tensor_tensor(MT[:, H // 2:H, :], mt_tmp[:],
                              causal.ap().unsqueeze(1).broadcast_to((128, H // 2, 128)), ALU.mult)
            eng.drain().then_inc(sems['VS'], 1)
        pub('VS'); VS_MT = cnt['VS']

        lnv = dict(rows=0, bc=0, sqfree=0)

        def ln_block(buf, c0, n, dep=None):
            if me('scalar'):
                if dep is not None:
                    wait(dep[0], dep[1])
                wait('TS', lnv['sqfree'])
                for dj in range(DC):
                    eng.activation(sq_sb[:, dj, 0:n], buf[:, dj, c0:c0 + n], AF.Square)
                eng.drain().then_inc(sems['SS'], 1)
            pub('SS'); ss_sq = cnt['SS']
            if me('tensor'):
                if dep is not None:
                    wait(dep[0], dep[1])
                wait('SS', ss_sq)
                wait('VS', lnv['rows'])
                for dj in range(DC):
                    eng.matmul(stx_ps[:, 0:n], ones_bf[:], buf[:, dj, c0:c0 + n],
                               start=(dj == 0), stop=(dj == DC - 1))
                for dj in range(DC):
                    inst = eng.matmul(sts_ps[:, 0:n], ones_bf[:], sq_sb[:, dj, 0:n],
                                      start=(dj == 0), stop=(dj == DC - 1))
                inst.then_inc(sems['TS'], 1)
            pub('TS'); ts_st = cnt['TS']; lnv['sqfree'] = ts_st
            if me('vector'):
                wait('TS', ts_st)
                eng.tensor_scalar(r0[:, 0:n], stx_ps[:, 0:n], 1.0 / D, None, ALU.mult)
                eng.tensor_scalar(r1[:, 0:n], sts_ps[:, 0:n], 1.0 / D, 1e-5, ALU.mult, ALU.add)
                eng.drain()
                eng.tensor_tensor(r2[:, 0:n], r0[:, 0:n], r0[:, 0:n], ALU.mult)
                eng.drain()
                eng.tensor_tensor(r1[:, 0:n], r1[:, 0:n], r2[:, 0:n], ALU.subtract)
                eng.drain().then_inc(sems['VS'], 1)
            pub('VS'); vs_var = cnt['VS']; lnv['rows'] = vs_var
            if me('scalar'):
                wait('VS', vs_var)
                eng.activation(r1[:, 0:n], r1[:, 0:n], AF.Ln)
                eng.drain()
                eng.activation(r1[:, 0:n], r1[:, 0:n], AF.Exp, scale=-0.5)
                eng.drain().then_inc(sems['SS'], 1)
            pub('SS'); ss_rs = cnt['SS']
            if me('vector'):
                wait('SS', ss_rs)
                eng.tensor_copy(arow[:, 0:n], r1[:, 0:n])
                eng.tensor_tensor(r2[:, 0:n], r0[:, 0:n], r1[:, 0:n], ALU.mult)
                eng.drain()
                eng.tensor_scalar(brow[:, 0:n], r2[:, 0:n], -1.0, None, ALU.mult)
                eng.drain().then_inc(sems['VS'], 1)
            pub('VS'); vs_ab = cnt['VS']
            if me('tensor'):
                wait('VS', vs_ab)
                wait('VS', lnv['bc'])
                eng.matmul(bca_ps[:, 0:n], onesb_row[:], arow[:, 0:n], start=True, stop=True)
                inst = eng.matmul(bcb_ps[:, 0:n], onesb_row[:], brow[:, 0:n], start=True, stop=True)
                inst.then_inc(sems['TS'], 1)
            pub('TS'); ts_bc = cnt['TS']
            if me('vector'):
                wait('TS', ts_bc)
                eng.tensor_copy(a_sb[:, 0:n], bca_ps[:, 0:n])
                eng.tensor_copy(b_sb[:, 0:n], bcb_ps[:, 0:n])
                eng.drain()
                for dj in range(DC):
                    eng.tensor_tensor(buf[:, dj, c0:c0 + n], buf[:, dj, c0:c0 + n],
                                      a_sb[:, 0:n], ALU.mult)
                eng.drain()
                for dj in range(DC):
                    eng.tensor_tensor(buf[:, dj, c0:c0 + n], buf[:, dj, c0:c0 + n],
                                      b_sb[:, 0:n], ALU.add)
                eng.drain().then_inc(sems['VS'], 1)
            pub('VS'); lnv['bc'] = cnt['VS']
            return cnt['VS']

        scn = dict(vA=0, tRD=0, vST={}, vRD=0, vstate=VS_state0, t0=0)

        def scan_chunk(buf, wk0, full, cast, pre_vs=0):
            if me('tensor'):
                wait('VS', scn['vA'])
                wait('GDP', GD_pre)
                if pre_vs:
                    wait('VS', pre_vs)
                for ih in range(2):
                    for dj in range(DC):
                        eng.matmul(v_ps[:, ih * 512:(ih + 1) * 512],
                                   buf[:, dj, wk0 + 1:wk0 + 1 + C],
                                   WwT_sb[:, dj, ih * 512:(ih + 1) * 512],
                                   start=(dj == 0), stop=(dj == DC - 1))
                for dc in range(DC):
                    inst = eng.transpose(
                        wk_ps.ap().rearrange("p (a c) -> p a c", a=DC)[:, dc, :],
                        buf[:, dc, wk0:wk0 + C], identb[:])
                inst.then_inc(sems['TS'], 1)
            pub('TS'); tA = cnt['TS']
            if me('vector'):
                wait('TS', tA)
                if scn['tRD']:
                    wait('TS', scn['tRD'])
                eng.tensor_copy(v_tm.ap().rearrange("p a c -> p (a c)"), v_ps[:])
                eng.tensor_copy(wk_tm.ap().rearrange("p a c -> p (a c)"), wk_ps[:])
                eng.drain()
                eng.tensor_tensor(vgw.ap().rearrange("p a c -> p (a c)"),
                                  v_tm.ap().rearrange("p a c -> p (a c)"),
                                  gwfull.ap().rearrange("p h i -> p (h i)"), ALU.mult)
                eng.drain().then_inc(sems['VS'], 1)
            pub('VS'); scn['vA'] = cnt['VS']
            vA = scn['vA']

            if full:
                for g in range(4):
                    heads = [4 * g + j for j in range(4)]
                    if me('tensor'):
                        wait('VS', scn['vST'].get(g - 2 if g >= 2 else g + 2, 0))
                        if g < 2:
                            wait('VS', VS_MT)
                        for j, h in enumerate(heads):
                            par, dcb = h % 2, h // 2
                            slot = (g % 2) * 256 + (j // 2) * 128
                            bank = st_pse if par == 0 else st_pso
                            inst = eng.matmul(bank[:, slot:slot + 128],
                                              buf[par * 64:par * 64 + 64, dcb, wk0:wk0 + C],
                                              buf[par * 64:par * 64 + 64, dcb, wk0 + 1:wk0 + 1 + C],
                                              start=True, stop=True)
                        inst.then_inc(sems['TS'], 1)
                    pub('TS'); tst = cnt['TS']
                    if me('vector'):
                        wait('TS', tst)
                        for j, h in enumerate(heads):
                            par = h % 2
                            slot = (g % 2) * 256 + (j // 2) * 128
                            bank = st_pse if par == 0 else st_pso
                            eng.tensor_tensor(stil[g][:, j, :], bank[:, slot:slot + 128],
                                              MT[:, h, :], ALU.mult)
                        for j, h in enumerate(heads):
                            par, dcb = h % 2, h // 2
                            eng.tensor_tensor(rkgp[par * 64:par * 64 + 64, dcb, :],
                                              buf[par * 64:par * 64 + 64, dcb, wk0 + 1:wk0 + 1 + C],
                                              gp_all[par * 64:par * 64 + 64, dcb, :], ALU.mult)
                        eng.drain().then_inc(sems['VS'], 1)
                    pub('VS'); scn['vST'][g] = cnt['VS']
                    if me('tensor'):
                        wait('VS', scn['vST'][g])
                        wait('VS', vA)
                        wait('VS', scn['vstate'])
                        if g == 0:
                            wait('VS', scn['vRD'])
                        rview = reads_ps.ap().rearrange("p (a c) -> p a c", a=DC)
                        dview = delta_ps.ap().rearrange("p (a c) -> p a c", a=DC)
                        for j, h in enumerate(heads):
                            par, dcb = h % 2, h // 2
                            pb = par * 64
                            eng.matmul(rview[pb:pb + 64, dcb, :], stateT_bf[:, h, :],
                                       rkgp[:, dcb, :], start=True, stop=False)
                            eng.matmul(rview[pb:pb + 64, dcb, :],
                                       v_tm[:, dcb, pb:pb + 64], stil[g][:, j, :],
                                       start=False, stop=True)
                            inst = eng.matmul(dview[pb:pb + 64, dcb, :],
                                              wk_tm[:, dcb, pb:pb + 64], vgw[:, dcb, pb:pb + 64],
                                              start=True, stop=True)
                        inst.then_inc(sems['TS'], 1)
                    pub('TS'); scn['tRD'] = cnt['TS']
            else:
                if me('tensor'):
                    wait('VS', vA)
                    wait('VS', scn['vstate'])
                    dview = delta_ps.ap().rearrange("p (a c) -> p a c", a=DC)
                    for h in range(H):
                        par, dcb = h % 2, h // 2
                        pb = par * 64
                        inst = eng.matmul(dview[pb:pb + 64, dcb, :],
                                          wk_tm[:, dcb, pb:pb + 64], vgw[:, dcb, pb:pb + 64],
                                          start=True, stop=True)
                    inst.then_inc(sems['TS'], 1)
                pub('TS'); scn['tRD'] = cnt['TS']

            if full:
                if me('vector'):
                    wait('TS', scn['tRD'])
                    rview = reads_ps.ap().rearrange("p (a c) -> p a c", a=DC)
                    for dcb in range(DC):
                        eng.tensor_scalar(reads_fm[:, dcb, scn['t0']:scn['t0'] + C],
                                          rview[:, dcb, :], al_pair[:, dcb:dcb + 1], None, ALU.mult)
                    eng.drain().then_inc(sems['VS'], 1)
                pub('VS'); scn['vRD'] = cnt['VS']

            if me('vector'):
                if not full:
                    wait('TS', scn['tRD'])
                eng.tensor_tensor(stateT.ap().rearrange("p a i -> p (a i)"),
                                  stateT.ap().rearrange("p a i -> p (a i)"),
                                  gCst.ap().rearrange("p a i -> p (a i)"), ALU.mult)
                eng.drain()
                eng.tensor_tensor(stateT.ap().rearrange("p a i -> p (a i)"),
                                  stateT.ap().rearrange("p a i -> p (a i)"),
                                  delta_ps[:], ALU.add)
                eng.drain()
                if cast:
                    eng.tensor_copy(stateT_bf[0:64, 0:H:2, :], stateT[0:64, :, :])
                    eng.tensor_copy(stateT_bf[64:128, 1:H:2, :], stateT[64:128, :, :])
                    eng.drain()
                eng.drain().then_inc(sems['VS'], 1)
            pub('VS'); scn['vstate'] = cnt['VS']

        xp_free = {}
        vs_lkb = {}
        for b in range(NBP):
            slot = b % 2
            if me('gpsimd'):
                if b >= 2:
                    wait('TS', xp_free[b - 2])
                if b >= 1 and (b - 1) in vs_lkb:
                    wait('VS', vs_lkb[b - 1])
                eng.dma_start(
                    out=xp_sl[:, slot, :, :],
                    in_=xTp.rearrange("(a p) t -> p a t", p=128)[:, :, b * LB:b * LB + LB + 1]
                ).then_inc(sems['GX%d' % slot], 16)
            pub('GX%d' % slot, 16); gdb = cnt['GX%d' % slot]
            buf = xp_sl.ap()[:, slot, :, :]
            vs_lnb = ln_block(buf, 1, LB, dep=('GX%d' % slot, gdb))
            lkb = 0
            if b >= 1:
                if me('vector'):
                    eng.tensor_copy(xp_sl[:, slot, :, 0:1],
                                    xp_sl[:, 1 - slot, :, LB:LB + 1])
                    eng.drain().then_inc(sems['VS'], 1)
                lkb = pub('VS')
                vs_lkb[b] = lkb
            for ci in range(CPB):
                last = (b == NBP - 1 and ci == CPB - 1)
                scan_chunk(buf, ci * C, full=False, cast=last,
                           pre_vs=(max(lkb, vs_lnb) if ci == 0 else 0))
            xp_free[b] = scn['tRD']

        ln_block(xln_m.ap(), 0, 1, dep=('GDP', GD_pre))
        for b in range(NBM):
            ln_block(xln_m.ap(), 1 + b * LB, LB)
        VS_lnm = lnv['bc']
        if me('tensor'):
            wait('VS', VS_lnm)
        for m in range(NCHT):
            scn['t0'] = m * C
            scan_chunk(xln_m.ap(), m * C, full=True, cast=True)
        VS_scan_end = cnt['VS']

        sd_x = {}
        ts_tr = {}
        vs_slot = {}
        vs_x2 = {}
        for m in range(NCHT):
            sl = m % 2
            if me('sync'):
                if m >= 2:
                    wait('VS', vs_slot[m - 2])
                    wait('SR', 16 * m)
                eng.dma_start(out=xres_t[sl][:],
                              in_=x_tm[m * 128:(m + 1) * 128, :]).then_inc(sems['SX%d' % sl], 16)
            pub('SX%d' % sl, 16); sd_x[m] = cnt['SX%d' % sl]
            if me('tensor'):
                if m == 0:
                    wait('VS', VS_scan_end)
                if m >= 2:
                    wait('VS', vs_slot[m - 2])
                for ih in range(2):
                    for dcb in range(DC):
                        eng.matmul(op_ps[:, sl, ih * 512:(ih + 1) * 512],
                                   reads_fm[:, dcb, m * 128:(m + 1) * 128],
                                   WrT_sb[:, dcb, ih * 512:(ih + 1) * 512],
                                   start=(dcb == 0), stop=False)
                    for q in range(4):
                        inst = eng.matmul(
                            op_ps[:, sl, ih * 512 + q * 128: ih * 512 + (q + 1) * 128],
                            xln_m[:, ih * 4 + q, m * 128 + 1: m * 128 + 129],
                            identb[:], start=False, stop=(q == 3))
                inst.then_inc(sems['TS'], 1)
            pub('TS'); ts_op = cnt['TS']
            if me('vector'):
                wait('TS', ts_op)
                wait('SX%d' % sl, sd_x[m])
                eng.tensor_tensor(xres_t[sl][:], op_ps.ap()[:, sl, :], xres_t[sl][:], ALU.add)
                eng.drain().then_inc(sems['VS'], 1)
            pub('VS'); vs_r0 = cnt['VS']
            if me('sync'):
                wait('VS', vs_r0)
                eng.dma_start(out=xres_hbm[m * 128:(m + 1) * 128, :],
                              in_=xres_t[sl][:]).then_inc(sems['SR'], 16)
            pub('SR', 16)
            if me('scalar'):
                wait('VS', vs_r0)
                eng.activation(sq2[:], xres_t[sl][:], AF.Square)
                eng.drain().then_inc(sems['SS'], 1)
            pub('SS'); ss2 = cnt['SS']
            if me('vector'):
                eng.reduce_sum(m0[:], xres_t[sl][:], axis=AX)
                eng.drain()
                eng.tensor_scalar(m0[:], m0[:], 1.0 / D, None, ALU.mult)
                wait('SS', ss2)
                eng.reduce_sum(m1c[:], sq2[:], axis=AX)
                eng.drain()
                eng.tensor_scalar(m1c[:], m1c[:], 1.0 / D, 1e-5, ALU.mult, ALU.add)
                eng.tensor_tensor(m2c[:], m0[:], m0[:], ALU.mult)
                eng.drain()
                eng.tensor_tensor(m1c[:], m1c[:], m2c[:], ALU.subtract)
                eng.drain().then_inc(sems['VS'], 1)
            pub('VS'); vs_v2 = cnt['VS']
            if me('scalar'):
                wait('VS', vs_v2)
                eng.activation(m1c[:], m1c[:], AF.Ln)
                eng.drain()
                eng.activation(m1c[:], m1c[:], AF.Exp, scale=-0.5)
                eng.drain().then_inc(sems['SS'], 1)
            pub('SS'); ss_r2 = cnt['SS']
            if me('vector'):
                wait('SS', ss_r2)
                if m >= 2:
                    wait('TS', ts_tr[m - 2])
                eng.tensor_scalar(xl2tm[sl][:], xres_t[sl][:], m0[:, 0:1], m1c[:, 0:1],
                                  ALU.subtract, ALU.mult)
                eng.drain().then_inc(sems['VS'], 1)
            pub('VS'); vs_slot[m] = cnt['VS']
            if me('tensor'):
                wait('VS', vs_slot[m])
                if m >= 2:
                    wait('VS', vs_x2[m - 2])
                for dj in range(DC):
                    inst = eng.transpose(tr_ps.ap()[:, sl, dj * 128:(dj + 1) * 128],
                                         xl2tm[sl][:, dj * 128:(dj + 1) * 128], identb[:])
                inst.then_inc(sems['TS'], 1)
            pub('TS'); ts_tr[m] = cnt['TS']
            if me('vector'):
                wait('TS', ts_tr[m])
                for dj in range(DC):
                    eng.tensor_copy(xln2_fm[:, dj, m * 128:(m + 1) * 128],
                                    tr_ps.ap()[:, sl, dj * 128:(dj + 1) * 128])
                eng.drain().then_inc(sems['VS'], 1)
            pub('VS'); vs_x2[m] = cnt['VS']
        VS_p3 = cnt['VS']
        TS_p3 = cnt['TS']

        vs_wprev = VS_p3
        ts_wtr = 0
        for tc in range(2 * NTC):
            if me('tensor'):
                wait('VS', vs_wprev)
                for dj in range(DC):
                    inst = eng.matmul(rt_ps[:, 0:E], xln2_fm[:, dj, tc * 128:(tc + 1) * 128],
                                      gateT_sb[:, dj, :], start=(dj == 0), stop=(dj == DC - 1))
                inst.then_inc(sems['TS'], 1)
            pub('TS'); ts_l = cnt['TS']
            if me('vector'):
                wait('TS', ts_l)
                eng.reduce_max(m0[:], rt_ps[:, 0:E], axis=AX)
                eng.drain()
                eng.tensor_scalar(sm_t[:], rt_ps[:, 0:E], m0[:, 0:1], None, ALU.subtract)
                eng.drain().then_inc(sems['VS'], 1)
            pub('VS'); vs_sm = cnt['VS']
            if me('scalar'):
                wait('VS', vs_sm)
                eng.activation(sm_t[:], sm_t[:], AF.Exp)
                eng.drain().then_inc(sems['SS'], 1)
            pub('SS'); ss_e = cnt['SS']
            if me('vector'):
                wait('SS', ss_e)
                eng.reduce_sum(m0[:], sm_t[:], axis=AX)
                eng.drain()
                eng.reciprocal(m1c[:], m0[:])
                eng.drain()
                eng.tensor_scalar(sm_t[:], sm_t[:], m1c[:, 0:1], None, ALU.mult)
                eng.drain()
                eng.reduce_max(m0[:], sm_t[:], axis=AX)
                eng.drain()
                eng.tensor_scalar(w_tm[:, tc, :], sm_t[:], m0[:, 0:1], None, ALU.is_lt)
                eng.drain()
                eng.tensor_tensor(w_tm[:, tc, :], w_tm[:, tc, :], sm_t[:], ALU.mult)
                eng.drain()
                eng.reduce_max(m1c[:], w_tm[:, tc, :], axis=AX)
                eng.drain()
                eng.tensor_scalar(w_tm[:, tc, :], sm_t[:], m1c[:, 0:1], None, ALU.is_ge)
                eng.drain()
                eng.tensor_tensor(w_tm[:, tc, :], w_tm[:, tc, :], sm_t[:], ALU.mult)
                eng.drain()
                wait('TS', ts_wtr)
                eng.tensor_copy(w_bfT[:], w_tm[:, tc, :])
                eng.drain().then_inc(sems['VS'], 1)
            pub('VS'); vs_wprev = cnt['VS']
            if me('tensor'):
                wait('VS', vs_wprev)
                eng.transpose(wf_ps[:, tc * 128:(tc + 1) * 128], w_bfT[:],
                              identb[:]).then_inc(sems['TS'], 1)
            pub('TS'); ts_wtr = cnt['TS']
        if me('vector'):
            wait('TS', ts_wtr)
            eng.tensor_copy(w_fm.ap().rearrange("p a c -> p (a c)"), wf_ps[:])
            eng.drain().then_inc(sems['VS'], 1)
        pub('VS'); VS_route = cnt['VS']

        vs_pass_done = 0
        ts_y_last = 0
        gd_w1 = {}
        w1_free = {}
        gd_w2 = {}
        w2_free = {}
        w1_ld = 0
        w2_ld = 0
        sd_fin = {}
        vs_fin = {}
        for p in range(2):
            toff = p * TT
            tcoff = p * NTC
            vs_b2 = {}
            for i in range(NTC * 2):
                tc, dcl = i // 2, i % 2
                yb = y_ps.ap()[:, i % 2, :]
                if me('tensor'):
                    if i == 0:
                        wait('VS', VS_route)
                        if p == 1:
                            wait('VS', vs_pass_done)
                    if i >= 2:
                        wait('VS', vs_b2[i - 2])
                    eng.matmul(yb, w_fm[:, tcoff + tc, :],
                               b2_sb[:, dcl * 512:(dcl + 1) * 512],
                               start=True, stop=True).then_inc(sems['TS'], 1)
                pub('TS'); ts_b2 = cnt['TS']
                if me('vector'):
                    wait('TS', ts_b2)
                    if i == 0 and p == 1:
                        wait('VS', vs_pass_done)
                    eng.tensor_copy(y_acc[:, tc, dcl * 512:(dcl + 1) * 512], yb)
                    eng.drain().then_inc(sems['VS'], 1)
                pub('VS'); vs_b2[i] = cnt['VS']
            vs_yinit = cnt['VS']

            vs_yevac = {}
            iy_g = 0
            ss_gelu = {}
            ih_g = 0
            for e in range(E):
                for fb in range(2):
                    fcol0 = fb * FB
                    slot_ids = []
                    for s in range(NS1):
                        slot = w1_ld % 2
                        slot_ids.append(slot)
                        if me('gpsimd'):
                            if w1_ld >= 2:
                                wait('TS', w1_free[w1_ld - 2])
                            else:
                                wait('TS', TS_p3)
                            eng.dma_start(
                                out=w1b[:, slot, :, :],
                                in_=w1t.rearrange("e (a p) f -> e p a f", p=128)[
                                    e, :, :, fcol0 + s * W1S * 128: fcol0 + (s + 1) * W1S * 128]
                            ).then_inc(sems['GW1%d' % slot], 16)
                        pub('GW1%d' % slot, 16)
                        gd_w1[w1_ld] = ('GW1%d' % slot, cnt['GW1%d' % slot])
                        w1_ld += 1
                    first_h = True
                    for fi in range(NF):
                        s_idx = fi // W1S
                        for th in range(NTH):
                            hb = h_ps.ap()[:, ih_g % 2, :]
                            if me('tensor'):
                                wait(*gd_w1[w1_ld - NS1 + s_idx])
                                if ih_g >= 2:
                                    wait('SS', ss_gelu[ih_g - 2])
                                for dj in range(DC):
                                    inst = eng.matmul(
                                        hb[:, 0:H512], w1b[:, slot_ids[s_idx], dj,
                                                (fi % W1S) * 128:(fi % W1S) * 128 + 128],
                                        xln2_fm[:, dj, toff + th * H512: toff + (th + 1) * H512],
                                        start=(dj == 0), stop=(dj == DC - 1))
                                inst.then_inc(sems['TS'], 1)
                            pub('TS'); ts_h = cnt['TS']
                            if fi % W1S == W1S - 1 and th == NTH - 1:
                                w1_free[w1_ld - NS1 + s_idx] = ts_h
                            if me('scalar'):
                                wait('TS', ts_h)
                                if first_h and ts_y_last:
                                    wait('TS', ts_y_last)
                                eng.activation(
                                    g_sb[:, fi, th * H512:(th + 1) * H512], hb[:, 0:H512],
                                    AF.Gelu_apprx_tanh,
                                    bias=b1_sb.ap()[:, e, fb * NF + fi:fb * NF + fi + 1])
                                first_h = False
                                eng.drain().then_inc(sems['SS'], 1)
                            pub('SS'); ss_gelu[ih_g] = cnt['SS']
                            ih_g += 1
                    ss_gelu_end = cnt['SS']
                    for dcl in range(2):
                        slot2 = w2_ld % 2
                        if me('gpsimd'):
                            if w2_ld >= 2:
                                wait('TS', w2_free[w2_ld - 2])
                            else:
                                wait('TS', TS_p3)
                            eng.dma_start(
                                out=w2b[:, slot2, :, :],
                                in_=w2t.rearrange("e (a p) d -> e p a d", p=128)[
                                    e, :, fb * NF:(fb + 1) * NF,
                                    dcl * 512:(dcl + 1) * 512]
                            ).then_inc(sems['GW2%d' % slot2], 16)
                        pub('GW2%d' % slot2, 16)
                        gd_w2[w2_ld] = ('GW2%d' % slot2, cnt['GW2%d' % slot2])
                        for tc in range(NTC):
                            yb = y_ps.ap()[:, iy_g % 2, :]
                            if me('tensor'):
                                wait(*gd_w2[w2_ld])
                                if tc == 0 and dcl == 0:
                                    wait('SS', ss_gelu_end)
                                    wait('VS', vs_yinit)
                                wait('VS', vs_yevac.get(iy_g - 2, 0))
                                for fi in range(NF):
                                    inst = eng.matmul(
                                        yb, g_sb[:, fi, tc * 128:(tc + 1) * 128],
                                        w2b[:, slot2, fi, :],
                                        start=(fi == 0), stop=(fi == NF - 1))
                                inst.then_inc(sems['TS'], 1)
                            pub('TS'); ts_y = cnt['TS']
                            if tc == NTC - 1:
                                w2_free[w2_ld] = ts_y
                            if tc == NTC - 1 and dcl == 1:
                                ts_y_last = ts_y
                            if me('vector'):
                                wait('TS', ts_y)
                                eng.scalar_tensor_tensor(
                                    y_acc[:, tc, dcl * 512:(dcl + 1) * 512], yb,
                                    w_tm.ap()[:, tcoff + tc, e:e + 1],
                                    y_acc[:, tc, dcl * 512:(dcl + 1) * 512],
                                    ALU.mult, ALU.add)
                                eng.drain().then_inc(sems['VS'], 1)
                            pub('VS'); vs_yevac[iy_g] = cnt['VS']
                            iy_g += 1
                        w2_ld += 1
            vs_y_end = cnt['VS']

            for tc in range(NTC):
                fidx = p * NTC + tc
                sl = fidx % 2
                if me('sync'):
                    if fidx >= 2:
                        wait('VS', vs_fin[fidx - 2])
                    else:
                        wait('VS', VS_p3)
                    wait('SR', 16 * NCHT)
                    eng.dma_start(out=xres_t[sl][:],
                                  in_=xres_hbm[toff + tc * 128: toff + (tc + 1) * 128, :]
                                  ).then_inc(sems['SX%d' % sl], 16)
                pub('SX%d' % sl, 16); sd_f = cnt['SX%d' % sl]
                if me('vector'):
                    wait('SX%d' % sl, sd_f)
                    wait('VS', vs_y_end)
                    if fidx >= 2:
                        wait(*sd_fin[fidx - 2])
                    eng.tensor_tensor(outb[sl][:], y_acc[:, tc, :], xres_t[sl][:], ALU.add)
                    eng.drain().then_inc(sems['VS'], 1)
                pub('VS'); vs_fin[fidx] = cnt['VS']
                if me('sync'):
                    wait('VS', vs_fin[fidx])
                    eng.dma_start(out=out[toff + tc * 128: toff + (tc + 1) * 128, :],
                                  in_=outb[sl][:]).then_inc(sems['SO%d' % sl], 16)
                pub('SO%d' % sl, 16); sd_fin[fidx] = ('SO%d' % sl, cnt['SO%d' % sl])
            vs_pass_done = cnt['VS']

        if me('sync'):
            for n in ["SDP", "SR", "SX0", "SX1", "SO0", "SO1"]:
                wait(n, cnt[n])
        if me('gpsimd'):
            for n in ["GDP", "GX0", "GX1", "GW10", "GW11", "GW20", "GW21"]:
                wait(n, cnt[n])
        return cnt

    @block.sync
    def _(e):
        stream('sync', e)

    @block.gpsimd
    def _(e):
        stream('gpsimd', e)

    @block.tensor
    def _(e):
        stream('tensor', e)

    @block.vector
    def _(e):
        stream('vector', e)

    @block.scalar
    def _(e):
        stream('scalar', e)

    blk.__exit__(None, None, None)
    return nc


def make_core_inputs(x_flat, n_cores, T, P, WwT, WrT, gateT, dcol, lacol,
                     w1t_a, w2t_a, b1r, b2_a, seq=None):
    Ttot = x_flat.shape[0]
    if seq is None:
        seq = Ttot // 2 if n_cores > 1 else Ttot
    xT = np.ascontiguousarray(x_flat.T)
    maps = []
    for c in range(n_cores):
        t0 = c * T
        bs = (t0 // seq) * seq
        p_c = t0 - bs
        xTp = np.zeros((D, P + 1), np.float32)
        if p_c:
            xTp[:, P + 1 - p_c:] = xT[:, bs:t0]
        xTm = np.zeros((D, T + 1), np.float32)
        xTm[:, 1:] = xT[:, t0:t0 + T]
        if p_c:
            xTm[:, 0] = xT[:, t0 - 1]
        maps.append(dict(
            xTm=xTm, xTp=xTp, x_tm=np.ascontiguousarray(x_flat[t0:t0 + T]),
            WwT=WwT, WrT=WrT, gateT=gateT, dcol=dcol, lacol=lacol,
            w1t=w1t_a, w2t=w2t_a, b1r=b1r, b2=b2_a))
    return maps


def kernel(L0_attn_out, L0_attn_W_write, L0_attn_W_read, L0_attn_decay,
           L0_attn_log_alpha, L0_ffn_gate_w, L0_ffn_w1, L0_ffn_b1,
           L0_ffn_w2, L0_ffn_b2):
    x = np.ascontiguousarray(np.asarray(L0_attn_out, np.float32).reshape(-1, D))
    WwT = np.ascontiguousarray(np.asarray(L0_attn_W_write, np.float32).T)
    WrT = np.ascontiguousarray(np.asarray(L0_attn_W_read, np.float32).T)
    gT = np.ascontiguousarray(np.asarray(L0_ffn_gate_w, np.float32).T)
    dcol = np.ascontiguousarray(np.broadcast_to(
        np.asarray(L0_attn_decay, np.float32), (128, H)))
    lacol = np.ascontiguousarray(np.broadcast_to(
        np.asarray(L0_attn_log_alpha, np.float32), (128, H)))
    w1 = np.asarray(L0_ffn_w1, np.float32)
    w2 = np.asarray(L0_ffn_w2, np.float32)
    F_ = w1.shape[0]
    w1t_a = np.ascontiguousarray(w1.transpose(1, 2, 0))
    w2t_a = np.ascontiguousarray(w2.transpose(1, 2, 0))
    b1 = np.asarray(L0_ffn_b1, np.float32)
    b1r = np.ascontiguousarray(b1.T.reshape(F_ // 128, 128, E).transpose(1, 2, 0))
    b2_a = np.ascontiguousarray(np.asarray(L0_ffn_b2, np.float32))

    T, P = 2048, 6144
    nc = build_program(T=T, P=P, F=F_)
    maps = make_core_inputs(x, 8, T, P, WwT, WrT, gT, dcol, lacol,
                            w1t_a, w2t_a, b1r, b2_a)
    res = run_bass_kernel_spmd(nc, maps, core_ids=list(range(8)))
    outp = np.concatenate([r["out"] for r in res.results], axis=0)
    return outp.reshape(np.asarray(L0_attn_out).shape).astype(np.float32)


# version 05
# speedup vs baseline: 1.0697x; 1.0697x over previous
import numpy as np
import concourse.bass as bass
import concourse.mybir as mybir
from concourse.bass_utils import run_bass_kernel_spmd

F32 = mybir.dt.float32
BF16 = mybir.dt.bfloat16
AF = mybir.ActivationFunctionType
ALU = mybir.AluOpType
AX = mybir.AxisListType.X

D, E, H, C, DH, DC = 1024, 8, 16, 128, 64, 8


def build_program(T=2048, P=6144, F=4096, LB=512):
    assert T % LB == 0 and P % LB == 0 and LB % C == 0 and F % 512 == 0
    NBP, NBM = P // LB, T // LB
    CPB = LB // C
    NCHT = T // C
    TT = T // 2
    NTC = TT // 128
    FB = F // 2
    NF = FB // 128
    W1S = max(NF // 2, 1)
    NS1 = NF // W1S
    NTH = max(TT // 512, 1)
    H512 = TT // NTH

    nc = bass.Bass("TRN2", target_bir_lowering=False)
    xTm = nc.dram_tensor("xTm", [D, T + 1], F32, kind="ExternalInput").ap()
    xTp = nc.dram_tensor("xTp", [D, P + 1], F32, kind="ExternalInput").ap()
    x_tm = nc.dram_tensor("x_tm", [T, D], F32, kind="ExternalInput").ap()
    WwT = nc.dram_tensor("WwT", [D, D], F32, kind="ExternalInput").ap()
    WrT = nc.dram_tensor("WrT", [D, D], F32, kind="ExternalInput").ap()
    gateT = nc.dram_tensor("gateT", [D, E], F32, kind="ExternalInput").ap()
    dcol = nc.dram_tensor("dcol", [128, H], F32, kind="ExternalInput").ap()
    lacol = nc.dram_tensor("lacol", [128, H], F32, kind="ExternalInput").ap()
    w1t = nc.dram_tensor("w1t", [E, D, F], F32, kind="ExternalInput").ap()
    w2t = nc.dram_tensor("w2t", [E, F, D], F32, kind="ExternalInput").ap()
    b1r = nc.dram_tensor("b1r", [128, E, F // 128], F32, kind="ExternalInput").ap()
    b2 = nc.dram_tensor("b2", [E, D], F32, kind="ExternalInput").ap()
    out = nc.dram_tensor("out", [T, D], F32, kind="ExternalOutput").ap()
    xres_hbm = nc.dram_tensor("xres_hbm", [T, D], F32, kind="Internal").ap()

    def sb(name, shape, dt):
        g = nc.sbuf_tensor(name, shape, dt)
        return g, g.__enter__()

    def sbp(name, shape, dt):
        return sb(name, shape, dt)[1]

    identb = sbp("identb", [128, 128], BF16)
    w_tm = sbp("w_tm", [128, 2 * NTC, E], F32)
    w_bfT = sbp("w_bfT", [128, E], BF16)
    w_fm = sbp("w_fm", [E, 2 * NTC, 128], BF16)
    m0 = sbp("m0", [128, 1], F32)
    m1c = sbp("m1c", [128, 1], F32)
    m2c = sbp("m2c", [128, 1], F32)
    sm_t = sbp("sm_t", [128, E], F32)
    b1_sb = sbp("b1_sb", [128, E, F // 128], F32)
    b2_sb = sbp("b2_sb", [E, D], BF16)
    gateT_sb = sbp("gateT_sb", [128, DC, E], BF16)
    xres_t = [sbp(f"xres_t{i}", [128, D], F32) for i in range(2)]
    xl2tm = [sbp(f"xl2tm{i}", [128, D], BF16) for i in range(2)]
    sq2 = sbp("sq2", [128, D], BF16)
    outb = [sbp(f"outb{i}", [128, D], F32) for i in range(2)]

    gA = []

    def sba(name, shape, dt):
        g, t = sb(name, shape, dt)
        gA.append(g)
        return t

    causal = sba("causal", [128, 128], F32)
    iotaD = sba("iotaD", [128, 128], F32)
    iotaC = sba("iotaC", [128, 128], F32)
    iota_rev = sba("iota_rev", [128, 1], F32)
    ones_bf = sba("ones_bf", [128, 1], BF16)
    onesb_row = sba("onesb_row", [1, 128], BF16)
    dcol_sb = sba("dcol_sb", [128, H], F32)
    lacol_sb = sba("lacol_sb", [128, H], F32)
    lg_cols = sba("lg_cols", [128, H], F32)
    al_cols = sba("al_cols", [128, H], F32)
    al_pair = sba("al_pair", [128, DC], F32)
    lgsel = sba("lgsel", [128, DC], F32)
    lgfull = sba("lgfull", [128, H, DH], F32)
    gCst = sba("gCst", [128, DC, DH], F32)
    gwfull = sba("gwfull", [128, H, DH], BF16)
    MT = sba("MT", [128, H, 128], BF16)
    gp_all = sba("gp_all", [128, DC, 128], BF16)
    gw_tmp = sba("gw_tmp", [128, H, DH], F32)
    gp_tmp = sba("gp_tmp", [128, DC, 128], F32)
    stateT = sba("stateT", [128, DC, DH], F32)
    stateT_bf = sba("stateT_bf", [128, H, DH], BF16)
    v_tm = sba("v_tm", [128, DC, 128], BF16)
    vgw = sba("vgw", [128, DC, 128], BF16)
    wk_tm = sba("wk_tm", [128, DC, 128], BF16)
    stil = [sba(f"stil{g}", [128, 4, 128], BF16) for g in range(4)]
    rkgp = sba("rkgp", [128, DC, 128], BF16)
    arow = sba("arow", [1, LB], BF16)
    brow = sba("brow", [1, LB], BF16)
    a_sb = sba("a_sb", [128, LB], BF16)
    b_sb = sba("b_sb", [128, LB], BF16)
    sq_sb = sba("sq_sb", [128, DC, LB], BF16)
    r0 = sba("r0", [1, LB], F32)
    r1 = sba("r1", [1, LB], F32)
    r2 = sba("r2", [1, LB], F32)
    xln_m = sba("xln_m", [128, DC, T + 8], BF16)
    reads_fm = sba("reads_fm", [128, DC, T], BF16)
    WwT_sb = sba("WwT_sb", [128, DC, D], BF16)
    WrT_sb = sba("WrT_sb", [128, DC, D], BF16)
    xp_sl = sba("xp_sl", [128, 2, DC, LB + 1], BF16)
    mt_tmp = sba("mt_tmp", [128, H // 2, 128], F32)
    for g in reversed(gA):
        g.__exit__(None, None, None)
    gB = []
    gg, xln2_fm = sb("xln2_fm", [128, DC, T], BF16); gB.append(gg)
    gg, g_sb = sb("g_sb", [128, NF, TT], BF16); gB.append(gg)
    gg, w1b = sb("w1b", [128, 2, DC, W1S * 128], BF16); gB.append(gg)
    gg, w2b = sb("w2b", [128, 2, NF, 512], BF16); gB.append(gg)
    gg, y_acc = sb("y_acc", [128, NTC, D], F32); gB.append(gg)
    for g in reversed(gB):
        g.__exit__(None, None, None)

    def psm(name, shape, dt):
        g = nc.psum_tensor(name, shape, dt)
        return g, g.__enter__()

    gPA = []
    gg, v_ps = psm("v_ps", [128, 1024], F32); gPA.append(gg)
    gg, wk_ps = psm("wk_ps", [128, 1024], BF16); gPA.append(gg)
    gg, delta_ps = psm("delta_ps", [128, 512], F32); gPA.append(gg)
    gLN = []
    gg, stx_ps = psm("stx_ps", [1, 512], F32); gLN.append(gg)
    gg, sts_ps = psm("sts_ps", [1, 512], F32); gLN.append(gg)
    gg, bca_ps = psm("bca_ps", [128, 512], F32); gLN.append(gg)
    gg, bcb_ps = psm("bcb_ps", [128, 512], F32); gLN.append(gg)
    for g in reversed(gLN):
        g.__exit__(None, None, None)
    gMS = []
    gg, st_pse = psm("st_pse", [128, 512], F32); gMS.append(gg)
    gg, st_pso = psm("st_pso", [128, 512], F32); gMS.append(gg)
    gg, reads_ps = psm("reads_ps", [128, 1024], F32); gMS.append(gg)
    for g in reversed(gMS):
        g.__exit__(None, None, None)
    for g in reversed(gPA):
        g.__exit__(None, None, None)
    gP3 = []
    gg, op_ps = psm("op_ps", [128, 2, 1024], F32); gP3.append(gg)
    gg, tr_ps = psm("tr_ps", [128, 2, 1024], BF16); gP3.append(gg)
    for g in reversed(gP3):
        g.__exit__(None, None, None)
    gP4 = []
    gg, h_ps = psm("h_ps", [128, 2, 512], F32); gP4.append(gg)
    gg, y_ps = psm("y_ps", [128, 2, 512], F32); gP4.append(gg)
    gg, rt_ps = psm("rt_ps", [128, 512], F32); gP4.append(gg)
    gg, wf_ps = psm("wf_ps", [8, 2 * NTC * 128], BF16); gP4.append(gg)
    for g in reversed(gP4):
        g.__exit__(None, None, None)

    sems = {}
    for n in ["TS", "VS", "SS", "GC", "SDP", "GDP", "SR",
              "GX0", "GX1", "SX0", "SX1", "SO0", "SO1",
              "GW10", "GW11", "GW20", "GW21"]:
        g = nc.semaphore(n)
        sems[n] = g.__enter__()
    blk = nc.Block()
    block = blk.__enter__()

    def stream(which, eng):
        cnt = {k: 0 for k in sems}

        def pub(name, n=1):
            cnt[name] += n
            return cnt[name]

        def wait(name, val):
            if eng is not None and val > 0:
                eng.wait_ge(sems[name], val)

        def me(o):
            return which == o

        if me('gpsimd'):
            eng.memset(ones_bf[:], 1.0)
            eng.memset(onesb_row[:], 1.0)
            eng.iota(iotaD[:], pattern=[[1, 128]], base=-1, channel_multiplier=-1,
                     allow_small_or_imprecise_dtypes=True)
            eng.drain()
            eng.tensor_scalar_max(iotaD[:], iotaD[:], 0.0)
            eng.iota(iotaC[:], pattern=[[1, 128]], base=0, channel_multiplier=0,
                     allow_small_or_imprecise_dtypes=True)
            eng.iota(iota_rev[:], pattern=[[0, 1]], base=C - 1, channel_multiplier=-1,
                     allow_small_or_imprecise_dtypes=True)
            eng.memset(causal[:], 0.0)
            eng.memset(identb[:], 0.0)
            eng.memset(xln_m[:, :, T + 1:T + 8], 0.0)
            eng.drain()
            eng.affine_select(out=causal[:], in_=causal[:], compare_op=ALU.is_ge,
                              fill=1.0, base=0, pattern=[[-1, 128]], channel_multiplier=1)
            eng.affine_select(out=identb[:], in_=identb[:], compare_op=ALU.not_equal,
                              fill=1.0, base=0, pattern=[[-1, 128]], channel_multiplier=1)
            eng.drain().then_inc(sems['GC'], 1)
        pub('GC')
        GC_c = cnt['GC']

        if me('gpsimd'):
            eng.dma_start(out=xln_m[:, :, 0:T + 1],
                          in_=xTm.rearrange("(a p) t -> p a t", p=128)).then_inc(sems['GDP'], 16)
            eng.dma_start(out=WwT_sb[:],
                          in_=WwT.rearrange("(a p) i -> p a i", p=128)).then_inc(sems['GDP'], 16)
            eng.dma_start(out=WrT_sb[:],
                          in_=WrT.rearrange("(a p) i -> p a i", p=128)).then_inc(sems['GDP'], 16)
            eng.dma_start(out=gateT_sb[:],
                          in_=gateT.rearrange("(a p) e -> p a e", p=128)).then_inc(sems['GDP'], 16)
            eng.dma_start(out=b2_sb[:], in_=b2).then_inc(sems['GDP'], 16)
        pub('GDP', 16 * 5)
        GD_pre = cnt['GDP']

        if me('sync'):
            eng.dma_start(out=dcol_sb[:], in_=dcol).then_inc(sems['SDP'], 16)
            eng.dma_start(out=lacol_sb[:], in_=lacol).then_inc(sems['SDP'], 16)
            eng.dma_start(out=b1_sb[:], in_=b1r).then_inc(sems['SDP'], 16)
        pub('SDP', 16 * 3)
        SD_pre = cnt['SDP']

        if me('scalar'):
            wait('SDP', SD_pre)
            eng.activation(lg_cols[:], dcol_sb[:], AF.Sigmoid)
            eng.drain()
            eng.activation(lg_cols[:], lg_cols[:], AF.Ln)
            eng.activation(al_cols[:], lacol_sb[:], AF.Exp)
            eng.drain().then_inc(sems['SS'], 1)
        pub('SS')
        SS_rows = cnt['SS']

        if me('vector'):
            wait('SS', SS_rows)
            wait('GC', GC_c)
            eng.tensor_copy(lgfull[:], lg_cols.ap().unsqueeze(2).broadcast_to((128, H, DH)))
            eng.tensor_copy(al_pair[0:64, :], al_cols[0:64, 0:H:2])
            eng.tensor_copy(al_pair[64:128, :], al_cols[64:128, 1:H:2])
            eng.tensor_copy(lgsel[0:64, :], lg_cols[0:64, 0:H:2])
            eng.tensor_copy(lgsel[64:128, :], lg_cols[64:128, 1:H:2])
            eng.drain()
            for h in range(H // 2):
                eng.tensor_scalar(mt_tmp[:, h, :], iotaD[:], lg_cols[:, h:h + 1], None, ALU.mult)
            eng.tensor_tensor(gp_tmp[:], iotaC.ap().unsqueeze(1).broadcast_to((128, DC, 128)),
                              lgsel.ap().unsqueeze(2).broadcast_to((128, DC, 128)), ALU.mult)
            eng.tensor_scalar(gw_tmp[:], lgfull[:], iota_rev[:, 0:1], None, ALU.mult)
            eng.drain().then_inc(sems['VS'], 1)
            eng.memset(stateT[:], 0.0)
            eng.memset(stateT_bf[:], 0.0)
            eng.drain().then_inc(sems['VS'], 1)
        pub('VS'); VS_pre1 = cnt['VS']
        pub('VS'); VS_state0 = cnt['VS']

        if me('scalar'):
            wait('VS', VS_pre1)
            eng.activation(mt_tmp[:], mt_tmp[:], AF.Exp)
            eng.activation(gp_all.ap(), gp_tmp[:], AF.Exp)
            eng.activation(gwfull.ap(), gw_tmp[:], AF.Exp)
            eng.activation(gCst[0:64, :, :], lgfull[0:64, 0:H:2, :], AF.Exp, scale=float(C))
            eng.activation(gCst[64:128, :, :], lgfull[64:128, 1:H:2, :], AF.Exp, scale=float(C))
            eng.drain().then_inc(sems['SS'], 1)
        pub('SS'); SS_exps = cnt['SS']

        if me('vector'):
            wait('SS', SS_exps)
            eng.tensor_tensor(MT[:, 0:H // 2, :], mt_tmp[:],
                              causal.ap().unsqueeze(1).broadcast_to((128, H // 2, 128)), ALU.mult)
            eng.drain()
            for h in range(H // 2, H):
                eng.tensor_scalar(mt_tmp[:, h - H // 2, :], iotaD[:],
                                  lg_cols[:, h:h + 1], None, ALU.mult)
            eng.drain().then_inc(sems['VS'], 1)
        pub('VS'); VS_mt2 = cnt['VS']
        if me('scalar'):
            wait('VS', VS_mt2)
            eng.activation(mt_tmp[:], mt_tmp[:], AF.Exp)
            eng.drain().then_inc(sems['SS'], 1)
        pub('SS'); SS_mt2 = cnt['SS']
        if me('vector'):
            wait('SS', SS_mt2)
            eng.tensor_tensor(MT[:, H // 2:H, :], mt_tmp[:],
                              causal.ap().unsqueeze(1).broadcast_to((128, H // 2, 128)), ALU.mult)
            eng.drain().then_inc(sems['VS'], 1)
        pub('VS'); VS_MT = cnt['VS']

        lnv = dict(rows=0, bc=0, sqfree=0)

        def ln_block(buf, c0, n, dep=None):
            if me('scalar'):
                if dep is not None:
                    wait(dep[0], dep[1])
                wait('TS', lnv['sqfree'])
                for dj in range(DC):
                    eng.activation(sq_sb[:, dj, 0:n], buf[:, dj, c0:c0 + n], AF.Square)
                eng.drain().then_inc(sems['SS'], 1)
            pub('SS'); ss_sq = cnt['SS']
            if me('tensor'):
                if dep is not None:
                    wait(dep[0], dep[1])
                wait('SS', ss_sq)
                wait('VS', lnv['rows'])
                for dj in range(DC):
                    eng.matmul(stx_ps[:, 0:n], ones_bf[:], buf[:, dj, c0:c0 + n],
                               start=(dj == 0), stop=(dj == DC - 1))
                for dj in range(DC):
                    inst = eng.matmul(sts_ps[:, 0:n], ones_bf[:], sq_sb[:, dj, 0:n],
                                      start=(dj == 0), stop=(dj == DC - 1))
                inst.then_inc(sems['TS'], 1)
            pub('TS'); ts_st = cnt['TS']; lnv['sqfree'] = ts_st
            if me('vector'):
                wait('TS', ts_st)
                eng.tensor_scalar(r0[:, 0:n], stx_ps[:, 0:n], 1.0 / D, None, ALU.mult)
                eng.tensor_scalar(r1[:, 0:n], sts_ps[:, 0:n], 1.0 / D, 1e-5, ALU.mult, ALU.add)
                eng.drain()
                eng.tensor_tensor(r2[:, 0:n], r0[:, 0:n], r0[:, 0:n], ALU.mult)
                eng.drain()
                eng.tensor_tensor(r1[:, 0:n], r1[:, 0:n], r2[:, 0:n], ALU.subtract)
                eng.drain().then_inc(sems['VS'], 1)
            pub('VS'); vs_var = cnt['VS']; lnv['rows'] = vs_var
            if me('scalar'):
                wait('VS', vs_var)
                eng.activation(r1[:, 0:n], r1[:, 0:n], AF.Ln)
                eng.drain()
                eng.activation(r1[:, 0:n], r1[:, 0:n], AF.Exp, scale=-0.5)
                eng.drain().then_inc(sems['SS'], 1)
            pub('SS'); ss_rs = cnt['SS']
            if me('vector'):
                wait('SS', ss_rs)
                eng.tensor_copy(arow[:, 0:n], r1[:, 0:n])
                eng.tensor_tensor(r2[:, 0:n], r0[:, 0:n], r1[:, 0:n], ALU.mult)
                eng.drain()
                eng.tensor_scalar(brow[:, 0:n], r2[:, 0:n], -1.0, None, ALU.mult)
                eng.drain().then_inc(sems['VS'], 1)
            pub('VS'); vs_ab = cnt['VS']
            if me('tensor'):
                wait('VS', vs_ab)
                wait('VS', lnv['bc'])
                eng.matmul(bca_ps[:, 0:n], onesb_row[:], arow[:, 0:n], start=True, stop=True)
                inst = eng.matmul(bcb_ps[:, 0:n], onesb_row[:], brow[:, 0:n], start=True, stop=True)
                inst.then_inc(sems['TS'], 1)
            pub('TS'); ts_bc = cnt['TS']
            if me('vector'):
                wait('TS', ts_bc)
                eng.tensor_copy(a_sb[:, 0:n], bca_ps[:, 0:n])
                eng.tensor_copy(b_sb[:, 0:n], bcb_ps[:, 0:n])
                eng.drain()
                for dj in range(DC):
                    eng.tensor_tensor(buf[:, dj, c0:c0 + n], buf[:, dj, c0:c0 + n],
                                      a_sb[:, 0:n], ALU.mult)
                eng.drain()
                for dj in range(DC):
                    eng.tensor_tensor(buf[:, dj, c0:c0 + n], buf[:, dj, c0:c0 + n],
                                      b_sb[:, 0:n], ALU.add)
                eng.drain().then_inc(sems['VS'], 1)
            pub('VS'); lnv['bc'] = cnt['VS']
            return cnt['VS']

        scn = dict(vA=0, sA=0, tRD=0, vST={}, vRD=0, vstate=VS_state0, t0=0)

        def scan_chunk(buf, wk0, full, cast, pre_vs=0):
            if me('tensor'):
                wait('SS', scn['sA'])
                wait('GDP', GD_pre)
                if pre_vs:
                    wait('VS', pre_vs)
                for ih in range(2):
                    for dj in range(DC):
                        eng.matmul(v_ps[:, ih * 512:(ih + 1) * 512],
                                   buf[:, dj, wk0 + 1:wk0 + 1 + C],
                                   WwT_sb[:, dj, ih * 512:(ih + 1) * 512],
                                   start=(dj == 0), stop=(dj == DC - 1))
                for dc in range(DC):
                    inst = eng.transpose(
                        wk_ps.ap().rearrange("p (a c) -> p a c", a=DC)[:, dc, :],
                        buf[:, dc, wk0:wk0 + C], identb[:])
                inst.then_inc(sems['TS'], 1)
            pub('TS'); tA = cnt['TS']
            if me('scalar'):
                wait('TS', tA)
                if scn['tRD']:
                    wait('TS', scn['tRD'])
                eng.copy(v_tm.ap().rearrange("p a c -> p (a c)"), v_ps[:])
                eng.copy(wk_tm.ap().rearrange("p a c -> p (a c)"), wk_ps[:])
                eng.drain().then_inc(sems['SS'], 1)
            pub('SS'); scn['sA'] = cnt['SS']
            if me('vector'):
                wait('SS', scn['sA'])
                eng.tensor_tensor(vgw.ap().rearrange("p a c -> p (a c)"),
                                  v_tm.ap().rearrange("p a c -> p (a c)"),
                                  gwfull.ap().rearrange("p h i -> p (h i)"), ALU.mult)
                eng.drain().then_inc(sems['VS'], 1)
            pub('VS'); scn['vA'] = cnt['VS']
            vA = scn['vA']

            if full:
                for g in range(4):
                    heads = [4 * g + j for j in range(4)]
                    if me('tensor'):
                        wait('VS', scn['vST'].get(g - 2 if g >= 2 else g + 2, 0))
                        if g < 2:
                            wait('VS', VS_MT)
                        for j, h in enumerate(heads):
                            par, dcb = h % 2, h // 2
                            slot = (g % 2) * 256 + (j // 2) * 128
                            bank = st_pse if par == 0 else st_pso
                            inst = eng.matmul(bank[:, slot:slot + 128],
                                              buf[par * 64:par * 64 + 64, dcb, wk0:wk0 + C],
                                              buf[par * 64:par * 64 + 64, dcb, wk0 + 1:wk0 + 1 + C],
                                              start=True, stop=True)
                        inst.then_inc(sems['TS'], 1)
                    pub('TS'); tst = cnt['TS']
                    if me('vector'):
                        wait('TS', tst)
                        for j, h in enumerate(heads):
                            par = h % 2
                            slot = (g % 2) * 256 + (j // 2) * 128
                            bank = st_pse if par == 0 else st_pso
                            eng.tensor_tensor(stil[g][:, j, :], bank[:, slot:slot + 128],
                                              MT[:, h, :], ALU.mult)
                        for j, h in enumerate(heads):
                            par, dcb = h % 2, h // 2
                            eng.tensor_tensor(rkgp[par * 64:par * 64 + 64, dcb, :],
                                              buf[par * 64:par * 64 + 64, dcb, wk0 + 1:wk0 + 1 + C],
                                              gp_all[par * 64:par * 64 + 64, dcb, :], ALU.mult)
                        eng.drain().then_inc(sems['VS'], 1)
                    pub('VS'); scn['vST'][g] = cnt['VS']
                    if me('tensor'):
                        wait('VS', scn['vST'][g])
                        wait('VS', vA)
                        wait('VS', scn['vstate'])
                        if g == 0:
                            wait('SS', scn['vRD'])
                        rview = reads_ps.ap().rearrange("p (a c) -> p a c", a=DC)
                        dview = delta_ps.ap().rearrange("p (a c) -> p a c", a=DC)
                        for j, h in enumerate(heads):
                            par, dcb = h % 2, h // 2
                            pb = par * 64
                            eng.matmul(rview[pb:pb + 64, dcb, :], stateT_bf[:, h, :],
                                       rkgp[:, dcb, :], start=True, stop=False)
                            eng.matmul(rview[pb:pb + 64, dcb, :],
                                       v_tm[:, dcb, pb:pb + 64], stil[g][:, j, :],
                                       start=False, stop=True)
                            inst = eng.matmul(dview[pb:pb + 64, dcb, :],
                                              wk_tm[:, dcb, pb:pb + 64], vgw[:, dcb, pb:pb + 64],
                                              start=True, stop=True)
                        inst.then_inc(sems['TS'], 1)
                    pub('TS'); scn['tRD'] = cnt['TS']
            else:
                if me('tensor'):
                    wait('VS', vA)
                    wait('VS', scn['vstate'])
                    dview = delta_ps.ap().rearrange("p (a c) -> p a c", a=DC)
                    for h in range(H):
                        par, dcb = h % 2, h // 2
                        pb = par * 64
                        inst = eng.matmul(dview[pb:pb + 64, dcb, :],
                                          wk_tm[:, dcb, pb:pb + 64], vgw[:, dcb, pb:pb + 64],
                                          start=True, stop=True)
                    inst.then_inc(sems['TS'], 1)
                pub('TS'); scn['tRD'] = cnt['TS']

            if full:
                if me('scalar'):
                    wait('TS', scn['tRD'])
                    rview = reads_ps.ap().rearrange("p (a c) -> p a c", a=DC)
                    for dcb in range(DC):
                        eng.activation(reads_fm[:, dcb, scn['t0']:scn['t0'] + C],
                                       rview[:, dcb, :], AF.Copy,
                                       scale=al_pair[:, dcb:dcb + 1])
                    eng.drain().then_inc(sems['SS'], 1)
                pub('SS'); scn['vRD'] = cnt['SS']

            if me('vector'):
                wait('TS', scn['tRD'])
                eng.tensor_tensor(stateT.ap().rearrange("p a i -> p (a i)"),
                                  stateT.ap().rearrange("p a i -> p (a i)"),
                                  gCst.ap().rearrange("p a i -> p (a i)"), ALU.mult)
                eng.drain()
                eng.tensor_tensor(stateT.ap().rearrange("p a i -> p (a i)"),
                                  stateT.ap().rearrange("p a i -> p (a i)"),
                                  delta_ps[:], ALU.add)
                eng.drain()
                if cast:
                    eng.tensor_copy(stateT_bf[0:64, 0:H:2, :], stateT[0:64, :, :])
                    eng.tensor_copy(stateT_bf[64:128, 1:H:2, :], stateT[64:128, :, :])
                    eng.drain()
                eng.drain().then_inc(sems['VS'], 1)
            pub('VS'); scn['vstate'] = cnt['VS']

        xp_free = {}
        vs_lkb = {}
        for b in range(NBP):
            slot = b % 2
            if me('gpsimd'):
                if b >= 2:
                    wait('TS', xp_free[b - 2])
                if b >= 1 and (b - 1) in vs_lkb:
                    wait('VS', vs_lkb[b - 1])
                eng.dma_start(
                    out=xp_sl[:, slot, :, :],
                    in_=xTp.rearrange("(a p) t -> p a t", p=128)[:, :, b * LB:b * LB + LB + 1]
                ).then_inc(sems['GX%d' % slot], 16)
            pub('GX%d' % slot, 16); gdb = cnt['GX%d' % slot]
            buf = xp_sl.ap()[:, slot, :, :]
            vs_lnb = ln_block(buf, 1, LB, dep=('GX%d' % slot, gdb))
            lkb = 0
            if b >= 1:
                if me('vector'):
                    eng.tensor_copy(xp_sl[:, slot, :, 0:1],
                                    xp_sl[:, 1 - slot, :, LB:LB + 1])
                    eng.drain().then_inc(sems['VS'], 1)
                lkb = pub('VS')
                vs_lkb[b] = lkb
            for ci in range(CPB):
                last = (b == NBP - 1 and ci == CPB - 1)
                scan_chunk(buf, ci * C, full=False, cast=last,
                           pre_vs=(max(lkb, vs_lnb) if ci == 0 else 0))
            xp_free[b] = scn['tRD']

        ln_block(xln_m.ap(), 0, 1, dep=('GDP', GD_pre))
        for b in range(NBM):
            ln_block(xln_m.ap(), 1 + b * LB, LB)
        VS_lnm = lnv['bc']
        if me('tensor'):
            wait('VS', VS_lnm)
        for m in range(NCHT):
            scn['t0'] = m * C
            scan_chunk(xln_m.ap(), m * C, full=True, cast=True)
        VS_scan_end = cnt['VS']
        SS_scan_end = cnt['SS']

        sd_x = {}
        ts_tr = {}
        vs_slot = {}
        vs_x2 = {}
        for m in range(NCHT):
            sl = m % 2
            if me('sync'):
                if m >= 2:
                    wait('VS', vs_slot[m - 2])
                    wait('SR', 16 * m)
                eng.dma_start(out=xres_t[sl][:],
                              in_=x_tm[m * 128:(m + 1) * 128, :]).then_inc(sems['SX%d' % sl], 16)
            pub('SX%d' % sl, 16); sd_x[m] = cnt['SX%d' % sl]
            if me('tensor'):
                if m == 0:
                    wait('VS', VS_scan_end)
                    wait('SS', SS_scan_end)
                if m >= 2:
                    wait('VS', vs_slot[m - 2])
                for ih in range(2):
                    for dcb in range(DC):
                        eng.matmul(op_ps[:, sl, ih * 512:(ih + 1) * 512],
                                   reads_fm[:, dcb, m * 128:(m + 1) * 128],
                                   WrT_sb[:, dcb, ih * 512:(ih + 1) * 512],
                                   start=(dcb == 0), stop=False)
                    for q in range(4):
                        inst = eng.matmul(
                            op_ps[:, sl, ih * 512 + q * 128: ih * 512 + (q + 1) * 128],
                            xln_m[:, ih * 4 + q, m * 128 + 1: m * 128 + 129],
                            identb[:], start=False, stop=(q == 3))
                inst.then_inc(sems['TS'], 1)
            pub('TS'); ts_op = cnt['TS']
            if me('vector'):
                wait('TS', ts_op)
                wait('SX%d' % sl, sd_x[m])
                eng.tensor_tensor(xres_t[sl][:], op_ps.ap()[:, sl, :], xres_t[sl][:], ALU.add)
                eng.drain().then_inc(sems['VS'], 1)
            pub('VS'); vs_r0 = cnt['VS']
            if me('sync'):
                wait('VS', vs_r0)
                eng.dma_start(out=xres_hbm[m * 128:(m + 1) * 128, :],
                              in_=xres_t[sl][:]).then_inc(sems['SR'], 16)
            pub('SR', 16)
            if me('scalar'):
                wait('VS', vs_r0)
                eng.activation(sq2[:], xres_t[sl][:], AF.Square)
                eng.drain().then_inc(sems['SS'], 1)
            pub('SS'); ss2 = cnt['SS']
            if me('vector'):
                eng.reduce_sum(m0[:], xres_t[sl][:], axis=AX)
                eng.drain()
                eng.tensor_scalar(m0[:], m0[:], 1.0 / D, None, ALU.mult)
                wait('SS', ss2)
                eng.reduce_sum(m1c[:], sq2[:], axis=AX)
                eng.drain()
                eng.tensor_scalar(m1c[:], m1c[:], 1.0 / D, 1e-5, ALU.mult, ALU.add)
                eng.tensor_tensor(m2c[:], m0[:], m0[:], ALU.mult)
                eng.drain()
                eng.tensor_tensor(m1c[:], m1c[:], m2c[:], ALU.subtract)
                eng.drain().then_inc(sems['VS'], 1)
            pub('VS'); vs_v2 = cnt['VS']
            if me('scalar'):
                wait('VS', vs_v2)
                eng.activation(m1c[:], m1c[:], AF.Ln)
                eng.drain()
                eng.activation(m1c[:], m1c[:], AF.Exp, scale=-0.5)
                eng.drain().then_inc(sems['SS'], 1)
            pub('SS'); ss_r2 = cnt['SS']
            if me('vector'):
                wait('SS', ss_r2)
                if m >= 2:
                    wait('TS', ts_tr[m - 2])
                eng.tensor_scalar(xl2tm[sl][:], xres_t[sl][:], m0[:, 0:1], m1c[:, 0:1],
                                  ALU.subtract, ALU.mult)
                eng.drain().then_inc(sems['VS'], 1)
            pub('VS'); vs_slot[m] = cnt['VS']
            if me('tensor'):
                wait('VS', vs_slot[m])
                if m >= 2:
                    wait('VS', vs_x2[m - 2])
                for dj in range(DC):
                    inst = eng.transpose(tr_ps.ap()[:, sl, dj * 128:(dj + 1) * 128],
                                         xl2tm[sl][:, dj * 128:(dj + 1) * 128], identb[:])
                inst.then_inc(sems['TS'], 1)
            pub('TS'); ts_tr[m] = cnt['TS']
            if me('vector'):
                wait('TS', ts_tr[m])
                for dj in range(DC):
                    eng.tensor_copy(xln2_fm[:, dj, m * 128:(m + 1) * 128],
                                    tr_ps.ap()[:, sl, dj * 128:(dj + 1) * 128])
                eng.drain().then_inc(sems['VS'], 1)
            pub('VS'); vs_x2[m] = cnt['VS']
        VS_p3 = cnt['VS']
        TS_p3 = cnt['TS']

        vs_wprev = VS_p3
        ts_wtr = 0
        for tc in range(2 * NTC):
            if me('tensor'):
                wait('VS', vs_wprev)
                for dj in range(DC):
                    inst = eng.matmul(rt_ps[:, 0:E], xln2_fm[:, dj, tc * 128:(tc + 1) * 128],
                                      gateT_sb[:, dj, :], start=(dj == 0), stop=(dj == DC - 1))
                inst.then_inc(sems['TS'], 1)
            pub('TS'); ts_l = cnt['TS']
            if me('vector'):
                wait('TS', ts_l)
                eng.reduce_max(m0[:], rt_ps[:, 0:E], axis=AX)
                eng.drain()
                eng.tensor_scalar(sm_t[:], rt_ps[:, 0:E], m0[:, 0:1], None, ALU.subtract)
                eng.drain().then_inc(sems['VS'], 1)
            pub('VS'); vs_sm = cnt['VS']
            if me('scalar'):
                wait('VS', vs_sm)
                eng.activation(sm_t[:], sm_t[:], AF.Exp)
                eng.drain().then_inc(sems['SS'], 1)
            pub('SS'); ss_e = cnt['SS']
            if me('vector'):
                wait('SS', ss_e)
                eng.reduce_sum(m0[:], sm_t[:], axis=AX)
                eng.drain()
                eng.reciprocal(m1c[:], m0[:])
                eng.drain()
                eng.tensor_scalar(sm_t[:], sm_t[:], m1c[:, 0:1], None, ALU.mult)
                eng.drain()
                eng.reduce_max(m0[:], sm_t[:], axis=AX)
                eng.drain()
                eng.tensor_scalar(w_tm[:, tc, :], sm_t[:], m0[:, 0:1], None, ALU.is_lt)
                eng.drain()
                eng.tensor_tensor(w_tm[:, tc, :], w_tm[:, tc, :], sm_t[:], ALU.mult)
                eng.drain()
                eng.reduce_max(m1c[:], w_tm[:, tc, :], axis=AX)
                eng.drain()
                eng.tensor_scalar(w_tm[:, tc, :], sm_t[:], m1c[:, 0:1], None, ALU.is_ge)
                eng.drain()
                eng.tensor_tensor(w_tm[:, tc, :], w_tm[:, tc, :], sm_t[:], ALU.mult)
                eng.drain()
                wait('TS', ts_wtr)
                eng.tensor_copy(w_bfT[:], w_tm[:, tc, :])
                eng.drain().then_inc(sems['VS'], 1)
            pub('VS'); vs_wprev = cnt['VS']
            if me('tensor'):
                wait('VS', vs_wprev)
                eng.transpose(wf_ps[:, tc * 128:(tc + 1) * 128], w_bfT[:],
                              identb[:]).then_inc(sems['TS'], 1)
            pub('TS'); ts_wtr = cnt['TS']
        if me('vector'):
            wait('TS', ts_wtr)
            eng.tensor_copy(w_fm.ap().rearrange("p a c -> p (a c)"), wf_ps[:])
            eng.drain().then_inc(sems['VS'], 1)
        pub('VS'); VS_route = cnt['VS']

        vs_pass_done = 0
        vs_yend_prev = 0
        ts_y_last = 0
        gd_w1 = {}
        w1_free = {}
        gd_w2 = {}
        w2_free = {}
        w1_ld = 0
        w2_ld = 0
        sd_fin = {}
        vs_fin = {}
        for p in range(2):
            toff = p * TT
            tcoff = p * NTC
            vs_b2 = {}
            for i in range(NTC * 2):
                tc, dcl = i // 2, i % 2
                yb = y_ps.ap()[:, i % 2, :]
                if me('tensor'):
                    if i == 0:
                        wait('VS', VS_route)
                        if p == 1:
                            wait('VS', vs_yend_prev)
                    if i >= 2:
                        wait('VS', vs_b2[i - 2])
                    eng.matmul(yb, w_fm[:, tcoff + tc, :],
                               b2_sb[:, dcl * 512:(dcl + 1) * 512],
                               start=True, stop=True).then_inc(sems['TS'], 1)
                pub('TS'); ts_b2 = cnt['TS']
                if me('vector'):
                    wait('TS', ts_b2)
                    if i == 0 and p == 1:
                        wait('VS', vs_pass_done)
                    eng.tensor_copy(y_acc[:, tc, dcl * 512:(dcl + 1) * 512], yb)
                    eng.drain().then_inc(sems['VS'], 1)
                pub('VS'); vs_b2[i] = cnt['VS']
            vs_yinit = cnt['VS']

            vs_yevac = {}
            iy_g = 0
            ss_gelu = {}
            ih_g = 0
            for e in range(E):
                for fb in range(2):
                    fcol0 = fb * FB
                    slot_ids = []
                    for s in range(NS1):
                        slot = w1_ld % 2
                        slot_ids.append(slot)
                        if me('gpsimd'):
                            if w1_ld >= 2:
                                wait('TS', w1_free[w1_ld - 2])
                            else:
                                wait('TS', TS_p3)
                            eng.dma_start(
                                out=w1b[:, slot, :, :],
                                in_=w1t.rearrange("e (a p) f -> e p a f", p=128)[
                                    e, :, :, fcol0 + s * W1S * 128: fcol0 + (s + 1) * W1S * 128]
                            ).then_inc(sems['GW1%d' % slot], 16)
                        pub('GW1%d' % slot, 16)
                        gd_w1[w1_ld] = ('GW1%d' % slot, cnt['GW1%d' % slot])
                        w1_ld += 1
                    first_h = True
                    for fi in range(NF):
                        s_idx = fi // W1S
                        for th in range(NTH):
                            hb = h_ps.ap()[:, ih_g % 2, :]
                            if me('tensor'):
                                wait(*gd_w1[w1_ld - NS1 + s_idx])
                                if ih_g >= 2:
                                    wait('SS', ss_gelu[ih_g - 2])
                                for dj in range(DC):
                                    inst = eng.matmul(
                                        hb[:, 0:H512], w1b[:, slot_ids[s_idx], dj,
                                                (fi % W1S) * 128:(fi % W1S) * 128 + 128],
                                        xln2_fm[:, dj, toff + th * H512: toff + (th + 1) * H512],
                                        start=(dj == 0), stop=(dj == DC - 1))
                                inst.then_inc(sems['TS'], 1)
                            pub('TS'); ts_h = cnt['TS']
                            if fi % W1S == W1S - 1 and th == NTH - 1:
                                w1_free[w1_ld - NS1 + s_idx] = ts_h
                            if me('scalar'):
                                wait('TS', ts_h)
                                if first_h and ts_y_last:
                                    wait('TS', ts_y_last)
                                eng.activation(
                                    g_sb[:, fi, th * H512:(th + 1) * H512], hb[:, 0:H512],
                                    AF.Gelu_apprx_tanh,
                                    bias=b1_sb.ap()[:, e, fb * NF + fi:fb * NF + fi + 1])
                                first_h = False
                                eng.drain().then_inc(sems['SS'], 1)
                            pub('SS'); ss_gelu[ih_g] = cnt['SS']
                            ih_g += 1
                    ss_gelu_end = cnt['SS']
                    for dcl in range(2):
                        slot2 = w2_ld % 2
                        if me('gpsimd'):
                            if w2_ld >= 2:
                                wait('TS', w2_free[w2_ld - 2])
                            else:
                                wait('TS', TS_p3)
                            eng.dma_start(
                                out=w2b[:, slot2, :, :],
                                in_=w2t.rearrange("e (a p) d -> e p a d", p=128)[
                                    e, :, fb * NF:(fb + 1) * NF,
                                    dcl * 512:(dcl + 1) * 512]
                            ).then_inc(sems['GW2%d' % slot2], 16)
                        pub('GW2%d' % slot2, 16)
                        gd_w2[w2_ld] = ('GW2%d' % slot2, cnt['GW2%d' % slot2])
                        for tc in range(NTC):
                            yb = y_ps.ap()[:, iy_g % 2, :]
                            if me('tensor'):
                                wait(*gd_w2[w2_ld])
                                if tc == 0 and dcl == 0:
                                    wait('SS', ss_gelu_end)
                                    wait('VS', vs_yinit)
                                wait('VS', vs_yevac.get(iy_g - 2, 0))
                                for fi in range(NF):
                                    inst = eng.matmul(
                                        yb, g_sb[:, fi, tc * 128:(tc + 1) * 128],
                                        w2b[:, slot2, fi, :],
                                        start=(fi == 0), stop=(fi == NF - 1))
                                inst.then_inc(sems['TS'], 1)
                            pub('TS'); ts_y = cnt['TS']
                            if tc == NTC - 1:
                                w2_free[w2_ld] = ts_y
                            if tc == NTC - 1 and dcl == 1:
                                ts_y_last = ts_y
                            if me('vector'):
                                wait('TS', ts_y)
                                eng.scalar_tensor_tensor(
                                    y_acc[:, tc, dcl * 512:(dcl + 1) * 512], yb,
                                    w_tm.ap()[:, tcoff + tc, e:e + 1],
                                    y_acc[:, tc, dcl * 512:(dcl + 1) * 512],
                                    ALU.mult, ALU.add)
                                eng.drain().then_inc(sems['VS'], 1)
                            pub('VS'); vs_yevac[iy_g] = cnt['VS']
                            iy_g += 1
                        w2_ld += 1
            vs_y_end = cnt['VS']

            for tc in range(NTC):
                fidx = p * NTC + tc
                sl = fidx % 2
                if me('sync'):
                    if fidx >= 2:
                        wait('VS', vs_fin[fidx - 2])
                    else:
                        wait('VS', VS_p3)
                    wait('SR', 16 * NCHT)
                    eng.dma_start(out=xres_t[sl][:],
                                  in_=xres_hbm[toff + tc * 128: toff + (tc + 1) * 128, :]
                                  ).then_inc(sems['SX%d' % sl], 16)
                pub('SX%d' % sl, 16); sd_f = cnt['SX%d' % sl]
                if me('vector'):
                    wait('SX%d' % sl, sd_f)
                    wait('VS', vs_y_end)
                    if fidx >= 2:
                        wait(*sd_fin[fidx - 2])
                    eng.tensor_tensor(outb[sl][:], y_acc[:, tc, :], xres_t[sl][:], ALU.add)
                    eng.drain().then_inc(sems['VS'], 1)
                pub('VS'); vs_fin[fidx] = cnt['VS']
                if me('sync'):
                    wait('VS', vs_fin[fidx])
                    eng.dma_start(out=out[toff + tc * 128: toff + (tc + 1) * 128, :],
                                  in_=outb[sl][:]).then_inc(sems['SO%d' % sl], 16)
                pub('SO%d' % sl, 16); sd_fin[fidx] = ('SO%d' % sl, cnt['SO%d' % sl])
            vs_pass_done = cnt['VS']
            vs_yend_prev = vs_y_end

        if me('sync'):
            for n in ["SDP", "SR", "SX0", "SX1", "SO0", "SO1"]:
                wait(n, cnt[n])
        if me('gpsimd'):
            for n in ["GDP", "GX0", "GX1", "GW10", "GW11", "GW20", "GW21"]:
                wait(n, cnt[n])
        return cnt

    @block.sync
    def _(e):
        stream('sync', e)

    @block.gpsimd
    def _(e):
        stream('gpsimd', e)

    @block.tensor
    def _(e):
        stream('tensor', e)

    @block.vector
    def _(e):
        stream('vector', e)

    @block.scalar
    def _(e):
        stream('scalar', e)

    blk.__exit__(None, None, None)
    return nc


def make_core_inputs(x_flat, n_cores, T, P, WwT, WrT, gateT, dcol, lacol,
                     w1t_a, w2t_a, b1r, b2_a, seq=None):
    Ttot = x_flat.shape[0]
    if seq is None:
        seq = Ttot // 2 if n_cores > 1 else Ttot
    xT = np.ascontiguousarray(x_flat.T)
    maps = []
    for c in range(n_cores):
        t0 = c * T
        bs = (t0 // seq) * seq
        p_c = t0 - bs
        xTp = np.zeros((D, P + 1), np.float32)
        if p_c:
            xTp[:, P + 1 - p_c:] = xT[:, bs:t0]
        xTm = np.zeros((D, T + 1), np.float32)
        xTm[:, 1:] = xT[:, t0:t0 + T]
        if p_c:
            xTm[:, 0] = xT[:, t0 - 1]
        maps.append(dict(
            xTm=xTm, xTp=xTp, x_tm=np.ascontiguousarray(x_flat[t0:t0 + T]),
            WwT=WwT, WrT=WrT, gateT=gateT, dcol=dcol, lacol=lacol,
            w1t=w1t_a, w2t=w2t_a, b1r=b1r, b2=b2_a))
    return maps


def kernel(L0_attn_out, L0_attn_W_write, L0_attn_W_read, L0_attn_decay,
           L0_attn_log_alpha, L0_ffn_gate_w, L0_ffn_w1, L0_ffn_b1,
           L0_ffn_w2, L0_ffn_b2):
    x = np.ascontiguousarray(np.asarray(L0_attn_out, np.float32).reshape(-1, D))
    WwT = np.ascontiguousarray(np.asarray(L0_attn_W_write, np.float32).T)
    WrT = np.ascontiguousarray(np.asarray(L0_attn_W_read, np.float32).T)
    gT = np.ascontiguousarray(np.asarray(L0_ffn_gate_w, np.float32).T)
    dcol = np.ascontiguousarray(np.broadcast_to(
        np.asarray(L0_attn_decay, np.float32), (128, H)))
    lacol = np.ascontiguousarray(np.broadcast_to(
        np.asarray(L0_attn_log_alpha, np.float32), (128, H)))
    w1 = np.asarray(L0_ffn_w1, np.float32)
    w2 = np.asarray(L0_ffn_w2, np.float32)
    F_ = w1.shape[0]
    w1t_a = np.ascontiguousarray(w1.transpose(1, 2, 0))
    w2t_a = np.ascontiguousarray(w2.transpose(1, 2, 0))
    b1 = np.asarray(L0_ffn_b1, np.float32)
    b1r = np.ascontiguousarray(b1.T.reshape(F_ // 128, 128, E).transpose(1, 2, 0))
    b2_a = np.ascontiguousarray(np.asarray(L0_ffn_b2, np.float32))

    T, P = 2048, 6144
    nc = build_program(T=T, P=P, F=F_)
    maps = make_core_inputs(x, 8, T, P, WwT, WrT, gT, dcol, lacol,
                            w1t_a, w2t_a, b1r, b2_a)
    res = run_bass_kernel_spmd(nc, maps, core_ids=list(range(8)))
    outp = np.concatenate([r["out"] for r in res.results], axis=0)
    return outp.reshape(np.asarray(L0_attn_out).shape).astype(np.float32)


# version 06
# speedup vs baseline: 1.0842x; 1.0135x over previous
import numpy as np
import concourse.bass as bass
import concourse.mybir as mybir
from concourse.bass_utils import run_bass_kernel_spmd

F32 = mybir.dt.float32
BF16 = mybir.dt.bfloat16
AF = mybir.ActivationFunctionType
ALU = mybir.AluOpType
AX = mybir.AxisListType.X

D, E, H, C, DH, DC = 1024, 8, 16, 128, 64, 8


def build_program(T=2048, P=6144, F=4096, LB=512):
    assert T % LB == 0 and P % LB == 0 and LB % C == 0 and F % 512 == 0
    NBP, NBM = P // LB, T // LB
    CPB = LB // C
    NCHT = T // C
    TT = T // 2
    NTC = TT // 128
    FB = F // 2
    NF = FB // 128
    W1S = max(NF // 2, 1)
    NS1 = NF // W1S
    NTH = max(TT // 512, 1)
    H512 = TT // NTH

    nc = bass.Bass("TRN2", target_bir_lowering=False)
    xTm = nc.dram_tensor("xTm", [D, T + 1], F32, kind="ExternalInput").ap()
    xTp = nc.dram_tensor("xTp", [D, P + 1], F32, kind="ExternalInput").ap()
    x_tm = nc.dram_tensor("x_tm", [T, D], F32, kind="ExternalInput").ap()
    WwT = nc.dram_tensor("WwT", [D, D], F32, kind="ExternalInput").ap()
    WrT = nc.dram_tensor("WrT", [D, D], F32, kind="ExternalInput").ap()
    gateT = nc.dram_tensor("gateT", [D, E], F32, kind="ExternalInput").ap()
    dcol = nc.dram_tensor("dcol", [128, H], F32, kind="ExternalInput").ap()
    lacol = nc.dram_tensor("lacol", [128, H], F32, kind="ExternalInput").ap()
    w1t = nc.dram_tensor("w1t", [E, D, F], F32, kind="ExternalInput").ap()
    w2t = nc.dram_tensor("w2t", [E, F, D], F32, kind="ExternalInput").ap()
    b1r = nc.dram_tensor("b1r", [128, E, F // 128], F32, kind="ExternalInput").ap()
    b2 = nc.dram_tensor("b2", [E, D], F32, kind="ExternalInput").ap()
    out = nc.dram_tensor("out", [T, D], F32, kind="ExternalOutput").ap()
    xres_hbm = nc.dram_tensor("xres_hbm", [T, D], F32, kind="Internal").ap()

    def sb(name, shape, dt):
        g = nc.sbuf_tensor(name, shape, dt)
        return g, g.__enter__()

    def sbp(name, shape, dt):
        return sb(name, shape, dt)[1]

    identb = sbp("identb", [128, 128], BF16)
    w_tm = sbp("w_tm", [128, 2 * NTC, E], F32)
    w_bfT = sbp("w_bfT", [128, E], BF16)
    w_fm = sbp("w_fm", [E, 2 * NTC, 128], BF16)
    m0 = sbp("m0", [128, 1], F32)
    m1c = sbp("m1c", [128, 1], F32)
    m2c = sbp("m2c", [128, 1], F32)
    sm_t = sbp("sm_t", [128, E], F32)
    b1_sb = sbp("b1_sb", [128, E, F // 128], F32)
    b2_sb = sbp("b2_sb", [E, D], BF16)
    gateT_sb = sbp("gateT_sb", [128, DC, E], BF16)
    xres_t = [sbp(f"xres_t{i}", [128, D], F32) for i in range(2)]
    xl2tm = [sbp(f"xl2tm{i}", [128, D], BF16) for i in range(2)]
    sq2 = sbp("sq2", [128, D], BF16)
    outb = [sbp(f"outb{i}", [128, D], F32) for i in range(2)]

    gA = []

    def sba(name, shape, dt):
        g, t = sb(name, shape, dt)
        gA.append(g)
        return t

    causal = sba("causal", [128, 128], F32)
    iotaD = sba("iotaD", [128, 128], F32)
    iotaC = sba("iotaC", [128, 128], F32)
    iota_rev = sba("iota_rev", [128, 1], F32)
    ones_bf = sba("ones_bf", [128, 1], BF16)
    onesb_row = sba("onesb_row", [1, 128], BF16)
    dcol_sb = sba("dcol_sb", [128, H], F32)
    lacol_sb = sba("lacol_sb", [128, H], F32)
    lg_cols = sba("lg_cols", [128, H], F32)
    al_cols = sba("al_cols", [128, H], F32)
    al_pair = sba("al_pair", [128, DC], F32)
    lgsel = sba("lgsel", [128, DC], F32)
    lgfull = sba("lgfull", [128, H, DH], F32)
    gCst = sba("gCst", [128, DC, DH], F32)
    gwfull = sba("gwfull", [128, H, DH], BF16)
    MT = sba("MT", [128, H, 128], BF16)
    gp_all = sba("gp_all", [128, DC, 128], BF16)
    gw_tmp = sba("gw_tmp", [128, H, DH], F32)
    gp_tmp = sba("gp_tmp", [128, DC, 128], F32)
    stateT = sba("stateT", [128, DC, DH], F32)
    stateT_bf = sba("stateT_bf", [128, H, DH], BF16)
    v_tm = sba("v_tm", [128, DC, 128], BF16)
    vgw = sba("vgw", [128, DC, 128], BF16)
    wk_tm = sba("wk_tm", [128, DC, 128], BF16)
    stil = [sba(f"stil{g}", [128, 4, 128], BF16) for g in range(4)]
    rkgp = sba("rkgp", [128, DC, 128], BF16)
    arow = sba("arow", [1, LB], BF16)
    brow = sba("brow", [1, LB], BF16)
    a_sb = sba("a_sb", [128, LB], BF16)
    b_sb = sba("b_sb", [128, LB], BF16)
    sq_sb = sba("sq_sb", [128, DC, LB], BF16)
    r0 = sba("r0", [1, LB], F32)
    r1 = sba("r1", [1, LB], F32)
    r2 = sba("r2", [1, LB], F32)
    xln_m = sba("xln_m", [128, DC, T + 8], BF16)
    reads_fm = sba("reads_fm", [128, DC, T], BF16)
    WwT_sb = sba("WwT_sb", [128, DC, D], BF16)
    WrT_sb = sba("WrT_sb", [128, DC, D], BF16)
    xp_sl = sba("xp_sl", [128, 2, DC, LB + 1], BF16)
    mt_tmp = sba("mt_tmp", [128, H // 2, 128], F32)
    for g in reversed(gA):
        g.__exit__(None, None, None)
    gB = []
    gg, xln2_fm = sb("xln2_fm", [128, DC, T], BF16); gB.append(gg)
    gg, g_sb = sb("g_sb", [128, NF, TT], BF16); gB.append(gg)
    gg, w1b = sb("w1b", [128, 2, DC, W1S * 128], BF16); gB.append(gg)
    gg, w2b = sb("w2b", [128, 2, NF, 512], BF16); gB.append(gg)
    gg, y_acc = sb("y_acc", [128, NTC, D], F32); gB.append(gg)
    for g in reversed(gB):
        g.__exit__(None, None, None)

    def psm(name, shape, dt):
        g = nc.psum_tensor(name, shape, dt)
        return g, g.__enter__()

    gPA = []
    gg, v_ps = psm("v_ps", [128, 1024], F32); gPA.append(gg)
    gg, wk_ps = psm("wk_ps", [128, 1024], BF16); gPA.append(gg)
    gg, delta_ps = psm("delta_ps", [128, 512], F32); gPA.append(gg)
    gLN = []
    gg, stx_ps = psm("stx_ps", [1, 512], F32); gLN.append(gg)
    gg, sts_ps = psm("sts_ps", [1, 512], F32); gLN.append(gg)
    gg, bca_ps = psm("bca_ps", [128, 512], F32); gLN.append(gg)
    gg, bcb_ps = psm("bcb_ps", [128, 512], F32); gLN.append(gg)
    for g in reversed(gLN):
        g.__exit__(None, None, None)
    gMS = []
    gg, st_pse = psm("st_pse", [128, 512], F32); gMS.append(gg)
    gg, st_pso = psm("st_pso", [128, 512], F32); gMS.append(gg)
    gg, reads_ps = psm("reads_ps", [128, 1024], F32); gMS.append(gg)
    for g in reversed(gMS):
        g.__exit__(None, None, None)
    for g in reversed(gPA):
        g.__exit__(None, None, None)
    gP3 = []
    gg, op_ps = psm("op_ps", [128, 2, 1024], F32); gP3.append(gg)
    gg, tr_ps = psm("tr_ps", [128, 2, 1024], BF16); gP3.append(gg)
    for g in reversed(gP3):
        g.__exit__(None, None, None)
    gP4 = []
    gg, h_ps = psm("h_ps", [128, 2, 512], F32); gP4.append(gg)
    gg, y_ps = psm("y_ps", [128, 2, 512], F32); gP4.append(gg)
    gg, rt_ps = psm("rt_ps", [128, 512], F32); gP4.append(gg)
    gg, wf_ps = psm("wf_ps", [8, 2 * NTC * 128], BF16); gP4.append(gg)
    for g in reversed(gP4):
        g.__exit__(None, None, None)

    sems = {}
    for n in ["TS", "VS", "SS", "GC", "SDP", "GDP", "SR",
              "GX0", "GX1", "SX0", "SX1", "SO0", "SO1",
              "GW10", "GW11", "GW20", "GW21"]:
        g = nc.semaphore(n)
        sems[n] = g.__enter__()
    blk = nc.Block()
    block = blk.__enter__()

    def stream(which, eng):
        cnt = {k: 0 for k in sems}

        def pub(name, n=1):
            cnt[name] += n
            return cnt[name]

        def wait(name, val):
            if eng is not None and val > 0:
                eng.wait_ge(sems[name], val)

        def me(o):
            return which == o

        if me('gpsimd'):
            eng.memset(ones_bf[:], 1.0)
            eng.memset(onesb_row[:], 1.0)
            eng.iota(iotaD[:], pattern=[[1, 128]], base=-1, channel_multiplier=-1,
                     allow_small_or_imprecise_dtypes=True)
            eng.drain()
            eng.tensor_scalar_max(iotaD[:], iotaD[:], 0.0)
            eng.iota(iotaC[:], pattern=[[1, 128]], base=0, channel_multiplier=0,
                     allow_small_or_imprecise_dtypes=True)
            eng.iota(iota_rev[:], pattern=[[0, 1]], base=C - 1, channel_multiplier=-1,
                     allow_small_or_imprecise_dtypes=True)
            eng.memset(causal[:], 0.0)
            eng.memset(identb[:], 0.0)
            eng.memset(xln_m[:, :, T + 1:T + 8], 0.0)
            eng.drain()
            eng.affine_select(out=causal[:], in_=causal[:], compare_op=ALU.is_ge,
                              fill=1.0, base=0, pattern=[[-1, 128]], channel_multiplier=1)
            eng.affine_select(out=identb[:], in_=identb[:], compare_op=ALU.not_equal,
                              fill=1.0, base=0, pattern=[[-1, 128]], channel_multiplier=1)
            eng.drain().then_inc(sems['GC'], 1)
        pub('GC')
        GC_c = cnt['GC']

        if me('gpsimd'):
            eng.dma_start(out=xln_m[:, :, 0:T + 1],
                          in_=xTm.rearrange("(a p) t -> p a t", p=128)).then_inc(sems['GDP'], 16)
            eng.dma_start(out=WwT_sb[:],
                          in_=WwT.rearrange("(a p) i -> p a i", p=128)).then_inc(sems['GDP'], 16)
            eng.dma_start(out=WrT_sb[:],
                          in_=WrT.rearrange("(a p) i -> p a i", p=128)).then_inc(sems['GDP'], 16)
            eng.dma_start(out=gateT_sb[:],
                          in_=gateT.rearrange("(a p) e -> p a e", p=128)).then_inc(sems['GDP'], 16)
            eng.dma_start(out=b2_sb[:], in_=b2).then_inc(sems['GDP'], 16)
        pub('GDP', 16 * 5)
        GD_pre = cnt['GDP']

        if me('sync'):
            eng.dma_start(out=dcol_sb[:], in_=dcol).then_inc(sems['SDP'], 16)
            eng.dma_start(out=lacol_sb[:], in_=lacol).then_inc(sems['SDP'], 16)
            eng.dma_start(out=b1_sb[:], in_=b1r).then_inc(sems['SDP'], 16)
        pub('SDP', 16 * 3)
        SD_pre = cnt['SDP']

        if me('scalar'):
            wait('SDP', SD_pre)
            eng.activation(lg_cols[:], dcol_sb[:], AF.Sigmoid)
            eng.drain()
            eng.activation(lg_cols[:], lg_cols[:], AF.Ln)
            eng.activation(al_cols[:], lacol_sb[:], AF.Exp)
            eng.drain().then_inc(sems['SS'], 1)
        pub('SS')
        SS_rows = cnt['SS']

        if me('vector'):
            wait('SS', SS_rows)
            wait('GC', GC_c)
            eng.tensor_copy(lgfull[:], lg_cols.ap().unsqueeze(2).broadcast_to((128, H, DH)))
            eng.tensor_copy(al_pair[0:64, :], al_cols[0:64, 0:H:2])
            eng.tensor_copy(al_pair[64:128, :], al_cols[64:128, 1:H:2])
            eng.tensor_copy(lgsel[0:64, :], lg_cols[0:64, 0:H:2])
            eng.tensor_copy(lgsel[64:128, :], lg_cols[64:128, 1:H:2])
            eng.drain()
            for h in range(H // 2):
                eng.tensor_scalar(mt_tmp[:, h, :], iotaD[:], lg_cols[:, h:h + 1], None, ALU.mult)
            eng.tensor_tensor(gp_tmp[:], iotaC.ap().unsqueeze(1).broadcast_to((128, DC, 128)),
                              lgsel.ap().unsqueeze(2).broadcast_to((128, DC, 128)), ALU.mult)
            eng.tensor_scalar(gw_tmp[:], lgfull[:], iota_rev[:, 0:1], None, ALU.mult)
            eng.drain().then_inc(sems['VS'], 1)
            eng.memset(stateT[:], 0.0)
            eng.memset(stateT_bf[:], 0.0)
            eng.drain().then_inc(sems['VS'], 1)
        pub('VS'); VS_pre1 = cnt['VS']
        pub('VS'); VS_state0 = cnt['VS']

        if me('scalar'):
            wait('VS', VS_pre1)
            eng.activation(mt_tmp[:], mt_tmp[:], AF.Exp)
            eng.activation(gp_all.ap(), gp_tmp[:], AF.Exp)
            eng.activation(gwfull.ap(), gw_tmp[:], AF.Exp)
            eng.activation(gCst[0:64, :, :], lgfull[0:64, 0:H:2, :], AF.Exp, scale=float(C))
            eng.activation(gCst[64:128, :, :], lgfull[64:128, 1:H:2, :], AF.Exp, scale=float(C))
            eng.drain().then_inc(sems['SS'], 1)
        pub('SS'); SS_exps = cnt['SS']

        if me('vector'):
            wait('SS', SS_exps)
            eng.tensor_tensor(MT[:, 0:H // 2, :], mt_tmp[:],
                              causal.ap().unsqueeze(1).broadcast_to((128, H // 2, 128)), ALU.mult)
            eng.drain()
            for h in range(H // 2, H):
                eng.tensor_scalar(mt_tmp[:, h - H // 2, :], iotaD[:],
                                  lg_cols[:, h:h + 1], None, ALU.mult)
            eng.drain().then_inc(sems['VS'], 1)
        pub('VS'); VS_mt2 = cnt['VS']
        if me('scalar'):
            wait('VS', VS_mt2)
            eng.activation(mt_tmp[:], mt_tmp[:], AF.Exp)
            eng.drain().then_inc(sems['SS'], 1)
        pub('SS'); SS_mt2 = cnt['SS']
        if me('vector'):
            wait('SS', SS_mt2)
            eng.tensor_tensor(MT[:, H // 2:H, :], mt_tmp[:],
                              causal.ap().unsqueeze(1).broadcast_to((128, H // 2, 128)), ALU.mult)
            eng.drain().then_inc(sems['VS'], 1)
        pub('VS'); VS_MT = cnt['VS']

        lnv = dict(rows=0, bc=0, sqfree=0)

        def ln_block(buf, c0, n, dep=None):
            if me('scalar'):
                if dep is not None:
                    wait(dep[0], dep[1])
                wait('TS', lnv['sqfree'])
                for dj in range(DC):
                    eng.activation(sq_sb[:, dj, 0:n], buf[:, dj, c0:c0 + n], AF.Square)
                eng.drain().then_inc(sems['SS'], 1)
            pub('SS'); ss_sq = cnt['SS']
            if me('tensor'):
                if dep is not None:
                    wait(dep[0], dep[1])
                wait('SS', ss_sq)
                wait('VS', lnv['rows'])
                for dj in range(DC):
                    eng.matmul(stx_ps[:, 0:n], ones_bf[:], buf[:, dj, c0:c0 + n],
                               start=(dj == 0), stop=(dj == DC - 1))
                for dj in range(DC):
                    inst = eng.matmul(sts_ps[:, 0:n], ones_bf[:], sq_sb[:, dj, 0:n],
                                      start=(dj == 0), stop=(dj == DC - 1))
                inst.then_inc(sems['TS'], 1)
            pub('TS'); ts_st = cnt['TS']; lnv['sqfree'] = ts_st
            if me('vector'):
                wait('TS', ts_st)
                eng.tensor_scalar(r0[:, 0:n], stx_ps[:, 0:n], 1.0 / D, None, ALU.mult)
                eng.tensor_scalar(r1[:, 0:n], sts_ps[:, 0:n], 1.0 / D, 1e-5, ALU.mult, ALU.add)
                eng.drain()
                eng.tensor_tensor(r2[:, 0:n], r0[:, 0:n], r0[:, 0:n], ALU.mult)
                eng.drain()
                eng.tensor_tensor(r1[:, 0:n], r1[:, 0:n], r2[:, 0:n], ALU.subtract)
                eng.drain().then_inc(sems['VS'], 1)
            pub('VS'); vs_var = cnt['VS']; lnv['rows'] = vs_var
            if me('scalar'):
                wait('VS', vs_var)
                eng.activation(r1[:, 0:n], r1[:, 0:n], AF.Ln)
                eng.drain()
                eng.activation(r1[:, 0:n], r1[:, 0:n], AF.Exp, scale=-0.5)
                eng.drain().then_inc(sems['SS'], 1)
            pub('SS'); ss_rs = cnt['SS']
            if me('vector'):
                wait('SS', ss_rs)
                eng.tensor_copy(arow[:, 0:n], r1[:, 0:n])
                eng.tensor_tensor(r2[:, 0:n], r0[:, 0:n], r1[:, 0:n], ALU.mult)
                eng.drain()
                eng.tensor_scalar(brow[:, 0:n], r2[:, 0:n], -1.0, None, ALU.mult)
                eng.drain().then_inc(sems['VS'], 1)
            pub('VS'); vs_ab = cnt['VS']
            if me('tensor'):
                wait('VS', vs_ab)
                wait('VS', lnv['bc'])
                eng.matmul(bca_ps[:, 0:n], onesb_row[:], arow[:, 0:n], start=True, stop=True)
                inst = eng.matmul(bcb_ps[:, 0:n], onesb_row[:], brow[:, 0:n], start=True, stop=True)
                inst.then_inc(sems['TS'], 1)
            pub('TS'); ts_bc = cnt['TS']
            if me('vector'):
                wait('TS', ts_bc)
                eng.tensor_copy(a_sb[:, 0:n], bca_ps[:, 0:n])
                eng.tensor_copy(b_sb[:, 0:n], bcb_ps[:, 0:n])
                eng.drain()
                for dj in range(DC):
                    eng.tensor_tensor(buf[:, dj, c0:c0 + n], buf[:, dj, c0:c0 + n],
                                      a_sb[:, 0:n], ALU.mult)
                eng.drain()
                for dj in range(DC):
                    eng.tensor_tensor(buf[:, dj, c0:c0 + n], buf[:, dj, c0:c0 + n],
                                      b_sb[:, 0:n], ALU.add)
                eng.drain().then_inc(sems['VS'], 1)
            pub('VS'); lnv['bc'] = cnt['VS']
            return cnt['VS']

        scn = dict(vA=0, sA=0, tRD=0, vST={}, vRD=0, vstate=VS_state0, t0=0)

        def scan_chunk(buf, wk0, full, cast, pre_vs=0):
            if me('tensor'):
                wait('VS', scn['vA'])
                wait('GDP', GD_pre)
                if pre_vs:
                    wait('VS', pre_vs)
                for ih in range(2):
                    for dj in range(DC):
                        eng.matmul(v_ps[:, ih * 512:(ih + 1) * 512],
                                   buf[:, dj, wk0 + 1:wk0 + 1 + C],
                                   WwT_sb[:, dj, ih * 512:(ih + 1) * 512],
                                   start=(dj == 0), stop=(dj == DC - 1))
                for dc in range(DC):
                    inst = eng.transpose(
                        wk_ps.ap().rearrange("p (a c) -> p a c", a=DC)[:, dc, :],
                        buf[:, dc, wk0:wk0 + C], identb[:])
                inst.then_inc(sems['TS'], 1)
            pub('TS'); tA = cnt['TS']
            if me('vector'):
                wait('TS', tA)
                if scn['tRD']:
                    wait('TS', scn['tRD'])
                eng.tensor_copy(v_tm.ap().rearrange("p a c -> p (a c)"), v_ps[:])
                eng.tensor_copy(wk_tm.ap().rearrange("p a c -> p (a c)"), wk_ps[:])
                eng.drain()
                eng.tensor_tensor(vgw.ap().rearrange("p a c -> p (a c)"),
                                  v_tm.ap().rearrange("p a c -> p (a c)"),
                                  gwfull.ap().rearrange("p h i -> p (h i)"), ALU.mult)
                eng.drain().then_inc(sems['VS'], 1)
            pub('VS'); scn['vA'] = cnt['VS']
            scn['sA'] = 0
            vA = scn['vA']

            if full:
                for g in range(4):
                    heads = [4 * g + j for j in range(4)]
                    if me('tensor'):
                        wait('VS', scn['vST'].get(g - 2 if g >= 2 else g + 2, 0))
                        if g < 2:
                            wait('VS', VS_MT)
                        for j, h in enumerate(heads):
                            par, dcb = h % 2, h // 2
                            slot = (g % 2) * 256 + (j // 2) * 128
                            bank = st_pse if par == 0 else st_pso
                            inst = eng.matmul(bank[:, slot:slot + 128],
                                              buf[par * 64:par * 64 + 64, dcb, wk0:wk0 + C],
                                              buf[par * 64:par * 64 + 64, dcb, wk0 + 1:wk0 + 1 + C],
                                              start=True, stop=True)
                        inst.then_inc(sems['TS'], 1)
                    pub('TS'); tst = cnt['TS']
                    if me('vector'):
                        wait('TS', tst)
                        for j, h in enumerate(heads):
                            par = h % 2
                            slot = (g % 2) * 256 + (j // 2) * 128
                            bank = st_pse if par == 0 else st_pso
                            eng.tensor_tensor(stil[g][:, j, :], bank[:, slot:slot + 128],
                                              MT[:, h, :], ALU.mult)
                        for j, h in enumerate(heads):
                            par, dcb = h % 2, h // 2
                            eng.tensor_tensor(rkgp[par * 64:par * 64 + 64, dcb, :],
                                              buf[par * 64:par * 64 + 64, dcb, wk0 + 1:wk0 + 1 + C],
                                              gp_all[par * 64:par * 64 + 64, dcb, :], ALU.mult)
                        eng.drain().then_inc(sems['VS'], 1)
                    pub('VS'); scn['vST'][g] = cnt['VS']
                    if me('tensor'):
                        wait('VS', scn['vST'][g])
                        wait('VS', vA)
                        wait('VS', scn['vstate'])
                        if g == 0:
                            wait('VS', scn['vRD'])
                        rview = reads_ps.ap().rearrange("p (a c) -> p a c", a=DC)
                        dview = delta_ps.ap().rearrange("p (a c) -> p a c", a=DC)
                        for j, h in enumerate(heads):
                            par, dcb = h % 2, h // 2
                            pb = par * 64
                            eng.matmul(rview[pb:pb + 64, dcb, :], stateT_bf[:, h, :],
                                       rkgp[:, dcb, :], start=True, stop=False)
                            eng.matmul(rview[pb:pb + 64, dcb, :],
                                       v_tm[:, dcb, pb:pb + 64], stil[g][:, j, :],
                                       start=False, stop=True)
                            inst = eng.matmul(dview[pb:pb + 64, dcb, :],
                                              wk_tm[:, dcb, pb:pb + 64], vgw[:, dcb, pb:pb + 64],
                                              start=True, stop=True)
                        inst.then_inc(sems['TS'], 1)
                    pub('TS'); scn['tRD'] = cnt['TS']
            else:
                if me('tensor'):
                    wait('VS', vA)
                    wait('VS', scn['vstate'])
                    dview = delta_ps.ap().rearrange("p (a c) -> p a c", a=DC)
                    for h in range(H):
                        par, dcb = h % 2, h // 2
                        pb = par * 64
                        inst = eng.matmul(dview[pb:pb + 64, dcb, :],
                                          wk_tm[:, dcb, pb:pb + 64], vgw[:, dcb, pb:pb + 64],
                                          start=True, stop=True)
                    inst.then_inc(sems['TS'], 1)
                pub('TS'); scn['tRD'] = cnt['TS']

            if full:
                if me('vector'):
                    wait('TS', scn['tRD'])
                    rview = reads_ps.ap().rearrange("p (a c) -> p a c", a=DC)
                    for dcb in range(DC):
                        eng.tensor_scalar(reads_fm[:, dcb, scn['t0']:scn['t0'] + C],
                                          rview[:, dcb, :], al_pair[:, dcb:dcb + 1], None, ALU.mult)
                    eng.drain().then_inc(sems['VS'], 1)
                pub('VS'); scn['vRD'] = cnt['VS']

            if me('vector'):
                wait('TS', scn['tRD'])
                eng.tensor_tensor(stateT.ap().rearrange("p a i -> p (a i)"),
                                  stateT.ap().rearrange("p a i -> p (a i)"),
                                  gCst.ap().rearrange("p a i -> p (a i)"), ALU.mult)
                eng.drain()
                eng.tensor_tensor(stateT.ap().rearrange("p a i -> p (a i)"),
                                  stateT.ap().rearrange("p a i -> p (a i)"),
                                  delta_ps[:], ALU.add)
                eng.drain()
                if cast:
                    eng.tensor_copy(stateT_bf[0:64, 0:H:2, :], stateT[0:64, :, :])
                    eng.tensor_copy(stateT_bf[64:128, 1:H:2, :], stateT[64:128, :, :])
                    eng.drain()
                eng.drain().then_inc(sems['VS'], 1)
            pub('VS'); scn['vstate'] = cnt['VS']

        xp_free = {}
        vs_lkb = {}
        for b in range(NBP):
            slot = b % 2
            if me('gpsimd'):
                if b >= 2:
                    wait('TS', xp_free[b - 2])
                if b >= 1 and (b - 1) in vs_lkb:
                    wait('VS', vs_lkb[b - 1])
                eng.dma_start(
                    out=xp_sl[:, slot, :, :],
                    in_=xTp.rearrange("(a p) t -> p a t", p=128)[:, :, b * LB:b * LB + LB + 1]
                ).then_inc(sems['GX%d' % slot], 16)
            pub('GX%d' % slot, 16); gdb = cnt['GX%d' % slot]
            buf = xp_sl.ap()[:, slot, :, :]
            vs_lnb = ln_block(buf, 1, LB, dep=('GX%d' % slot, gdb))
            lkb = 0
            if b >= 1:
                if me('vector'):
                    eng.tensor_copy(xp_sl[:, slot, :, 0:1],
                                    xp_sl[:, 1 - slot, :, LB:LB + 1])
                    eng.drain().then_inc(sems['VS'], 1)
                lkb = pub('VS')
                vs_lkb[b] = lkb
            for ci in range(CPB):
                last = (b == NBP - 1 and ci == CPB - 1)
                scan_chunk(buf, ci * C, full=False, cast=last,
                           pre_vs=(max(lkb, vs_lnb) if ci == 0 else 0))
            xp_free[b] = scn['tRD']

        ln_block(xln_m.ap(), 0, 1, dep=('GDP', GD_pre))
        for b in range(NBM):
            ln_block(xln_m.ap(), 1 + b * LB, LB)
        VS_lnm = lnv['bc']
        if me('tensor'):
            wait('VS', VS_lnm)
        for m in range(NCHT):
            scn['t0'] = m * C
            scan_chunk(xln_m.ap(), m * C, full=True, cast=True)
        VS_scan_end = cnt['VS']
        SS_scan_end = cnt['SS']

        sd_x = {}
        ts_tr = {}
        vs_slot = {}
        vs_x2 = {}
        for m in range(NCHT):
            sl = m % 2
            if me('sync'):
                if m >= 2:
                    wait('VS', vs_slot[m - 2])
                    wait('SR', 16 * m)
                eng.dma_start(out=xres_t[sl][:],
                              in_=x_tm[m * 128:(m + 1) * 128, :]).then_inc(sems['SX%d' % sl], 16)
            pub('SX%d' % sl, 16); sd_x[m] = cnt['SX%d' % sl]
            if me('tensor'):
                if m == 0:
                    wait('VS', VS_scan_end)
                    wait('SS', SS_scan_end)
                if m >= 2:
                    wait('VS', vs_slot[m - 2])
                for ih in range(2):
                    for dcb in range(DC):
                        eng.matmul(op_ps[:, sl, ih * 512:(ih + 1) * 512],
                                   reads_fm[:, dcb, m * 128:(m + 1) * 128],
                                   WrT_sb[:, dcb, ih * 512:(ih + 1) * 512],
                                   start=(dcb == 0), stop=False)
                    for q in range(4):
                        inst = eng.matmul(
                            op_ps[:, sl, ih * 512 + q * 128: ih * 512 + (q + 1) * 128],
                            xln_m[:, ih * 4 + q, m * 128 + 1: m * 128 + 129],
                            identb[:], start=False, stop=(q == 3))
                inst.then_inc(sems['TS'], 1)
            pub('TS'); ts_op = cnt['TS']
            if me('vector'):
                wait('TS', ts_op)
                wait('SX%d' % sl, sd_x[m])
                eng.tensor_tensor(xres_t[sl][:], op_ps.ap()[:, sl, :], xres_t[sl][:], ALU.add)
                eng.drain().then_inc(sems['VS'], 1)
            pub('VS'); vs_r0 = cnt['VS']
            if me('sync'):
                wait('VS', vs_r0)
                eng.dma_start(out=xres_hbm[m * 128:(m + 1) * 128, :],
                              in_=xres_t[sl][:]).then_inc(sems['SR'], 16)
            pub('SR', 16)
            if me('scalar'):
                wait('VS', vs_r0)
                eng.activation(sq2[:], xres_t[sl][:], AF.Square)
                eng.drain().then_inc(sems['SS'], 1)
            pub('SS'); ss2 = cnt['SS']
            if me('vector'):
                eng.reduce_sum(m0[:], xres_t[sl][:], axis=AX)
                eng.drain()
                eng.tensor_scalar(m0[:], m0[:], 1.0 / D, None, ALU.mult)
                wait('SS', ss2)
                eng.reduce_sum(m1c[:], sq2[:], axis=AX)
                eng.drain()
                eng.tensor_scalar(m1c[:], m1c[:], 1.0 / D, 1e-5, ALU.mult, ALU.add)
                eng.tensor_tensor(m2c[:], m0[:], m0[:], ALU.mult)
                eng.drain()
                eng.tensor_tensor(m1c[:], m1c[:], m2c[:], ALU.subtract)
                eng.drain().then_inc(sems['VS'], 1)
            pub('VS'); vs_v2 = cnt['VS']
            if me('scalar'):
                wait('VS', vs_v2)
                eng.activation(m1c[:], m1c[:], AF.Ln)
                eng.drain()
                eng.activation(m1c[:], m1c[:], AF.Exp, scale=-0.5)
                eng.drain().then_inc(sems['SS'], 1)
            pub('SS'); ss_r2 = cnt['SS']
            if me('vector'):
                wait('SS', ss_r2)
                if m >= 2:
                    wait('TS', ts_tr[m - 2])
                eng.tensor_scalar(xl2tm[sl][:], xres_t[sl][:], m0[:, 0:1], m1c[:, 0:1],
                                  ALU.subtract, ALU.mult)
                eng.drain().then_inc(sems['VS'], 1)
            pub('VS'); vs_slot[m] = cnt['VS']
            if me('tensor'):
                wait('VS', vs_slot[m])
                if m >= 2:
                    wait('VS', vs_x2[m - 2])
                for dj in range(DC):
                    inst = eng.transpose(tr_ps.ap()[:, sl, dj * 128:(dj + 1) * 128],
                                         xl2tm[sl][:, dj * 128:(dj + 1) * 128], identb[:])
                inst.then_inc(sems['TS'], 1)
            pub('TS'); ts_tr[m] = cnt['TS']
            if me('vector'):
                wait('TS', ts_tr[m])
                for dj in range(DC):
                    eng.tensor_copy(xln2_fm[:, dj, m * 128:(m + 1) * 128],
                                    tr_ps.ap()[:, sl, dj * 128:(dj + 1) * 128])
                eng.drain().then_inc(sems['VS'], 1)
            pub('VS'); vs_x2[m] = cnt['VS']
        VS_p3 = cnt['VS']
        TS_p3 = cnt['TS']

        vs_wprev = VS_p3
        ts_wtr = 0
        for tc in range(2 * NTC):
            if me('tensor'):
                wait('VS', vs_wprev)
                for dj in range(DC):
                    inst = eng.matmul(rt_ps[:, 0:E], xln2_fm[:, dj, tc * 128:(tc + 1) * 128],
                                      gateT_sb[:, dj, :], start=(dj == 0), stop=(dj == DC - 1))
                inst.then_inc(sems['TS'], 1)
            pub('TS'); ts_l = cnt['TS']
            if me('vector'):
                wait('TS', ts_l)
                eng.reduce_max(m0[:], rt_ps[:, 0:E], axis=AX)
                eng.drain()
                eng.tensor_scalar(sm_t[:], rt_ps[:, 0:E], m0[:, 0:1], None, ALU.subtract)
                eng.drain().then_inc(sems['VS'], 1)
            pub('VS'); vs_sm = cnt['VS']
            if me('scalar'):
                wait('VS', vs_sm)
                eng.activation(sm_t[:], sm_t[:], AF.Exp)
                eng.drain().then_inc(sems['SS'], 1)
            pub('SS'); ss_e = cnt['SS']
            if me('vector'):
                wait('SS', ss_e)
                eng.reduce_sum(m0[:], sm_t[:], axis=AX)
                eng.drain()
                eng.reciprocal(m1c[:], m0[:])
                eng.drain()
                eng.tensor_scalar(sm_t[:], sm_t[:], m1c[:, 0:1], None, ALU.mult)
                eng.drain()
                eng.reduce_max(m0[:], sm_t[:], axis=AX)
                eng.drain()
                eng.tensor_scalar(w_tm[:, tc, :], sm_t[:], m0[:, 0:1], None, ALU.is_lt)
                eng.drain()
                eng.tensor_tensor(w_tm[:, tc, :], w_tm[:, tc, :], sm_t[:], ALU.mult)
                eng.drain()
                eng.reduce_max(m1c[:], w_tm[:, tc, :], axis=AX)
                eng.drain()
                eng.tensor_scalar(w_tm[:, tc, :], sm_t[:], m1c[:, 0:1], None, ALU.is_ge)
                eng.drain()
                eng.tensor_tensor(w_tm[:, tc, :], w_tm[:, tc, :], sm_t[:], ALU.mult)
                eng.drain()
                wait('TS', ts_wtr)
                eng.tensor_copy(w_bfT[:], w_tm[:, tc, :])
                eng.drain().then_inc(sems['VS'], 1)
            pub('VS'); vs_wprev = cnt['VS']
            if me('tensor'):
                wait('VS', vs_wprev)
                eng.transpose(wf_ps[:, tc * 128:(tc + 1) * 128], w_bfT[:],
                              identb[:]).then_inc(sems['TS'], 1)
            pub('TS'); ts_wtr = cnt['TS']
        if me('vector'):
            wait('TS', ts_wtr)
            eng.tensor_copy(w_fm.ap().rearrange("p a c -> p (a c)"), wf_ps[:])
            eng.drain().then_inc(sems['VS'], 1)
        pub('VS'); VS_route = cnt['VS']

        vs_pass_done = 0
        vs_yend_prev = 0
        ts_y_last = 0
        gd_w1 = {}
        w1_free = {}
        gd_w2 = {}
        w2_free = {}
        w1_ld = 0
        w2_ld = 0
        sd_fin = {}
        vs_fin = {}
        for p in range(2):
            toff = p * TT
            tcoff = p * NTC
            vs_b2 = {}
            for i in range(NTC * 2):
                tc, dcl = i // 2, i % 2
                yb = y_ps.ap()[:, i % 2, :]
                if me('tensor'):
                    if i == 0:
                        wait('VS', VS_route)
                        if p == 1:
                            wait('VS', vs_yend_prev)
                    if i >= 2:
                        wait('VS', vs_b2[i - 2])
                    eng.matmul(yb, w_fm[:, tcoff + tc, :],
                               b2_sb[:, dcl * 512:(dcl + 1) * 512],
                               start=True, stop=True).then_inc(sems['TS'], 1)
                pub('TS'); ts_b2 = cnt['TS']
                if me('vector'):
                    wait('TS', ts_b2)
                    if i == 0 and p == 1:
                        wait('VS', vs_pass_done)
                    eng.tensor_copy(y_acc[:, tc, dcl * 512:(dcl + 1) * 512], yb)
                    eng.drain().then_inc(sems['VS'], 1)
                pub('VS'); vs_b2[i] = cnt['VS']
            vs_yinit = cnt['VS']

            vs_yevac = {}
            iy_g = 0
            ss_gelu = {}
            ih_g = 0
            for e in range(E):
                for fb in range(2):
                    fcol0 = fb * FB
                    slot_ids = []
                    for s in range(NS1):
                        slot = w1_ld % 2
                        slot_ids.append(slot)
                        if me('gpsimd'):
                            if w1_ld >= 2:
                                wait('TS', w1_free[w1_ld - 2])
                            else:
                                wait('TS', TS_p3)
                            eng.dma_start(
                                out=w1b[:, slot, :, :],
                                in_=w1t.rearrange("e (a p) f -> e p a f", p=128)[
                                    e, :, :, fcol0 + s * W1S * 128: fcol0 + (s + 1) * W1S * 128]
                            ).then_inc(sems['GW1%d' % slot], 16)
                        pub('GW1%d' % slot, 16)
                        gd_w1[w1_ld] = ('GW1%d' % slot, cnt['GW1%d' % slot])
                        w1_ld += 1
                    first_h = True
                    for fi in range(NF):
                        s_idx = fi // W1S
                        for th in range(NTH):
                            hb = h_ps.ap()[:, ih_g % 2, :]
                            if me('tensor'):
                                wait(*gd_w1[w1_ld - NS1 + s_idx])
                                if ih_g >= 2:
                                    wait('SS', ss_gelu[ih_g - 2])
                                for dj in range(DC):
                                    inst = eng.matmul(
                                        hb[:, 0:H512], w1b[:, slot_ids[s_idx], dj,
                                                (fi % W1S) * 128:(fi % W1S) * 128 + 128],
                                        xln2_fm[:, dj, toff + th * H512: toff + (th + 1) * H512],
                                        start=(dj == 0), stop=(dj == DC - 1))
                                inst.then_inc(sems['TS'], 1)
                            pub('TS'); ts_h = cnt['TS']
                            if fi % W1S == W1S - 1 and th == NTH - 1:
                                w1_free[w1_ld - NS1 + s_idx] = ts_h
                            if me('scalar'):
                                wait('TS', ts_h)
                                if first_h and ts_y_last:
                                    wait('TS', ts_y_last)
                                eng.activation(
                                    g_sb[:, fi, th * H512:(th + 1) * H512], hb[:, 0:H512],
                                    AF.Gelu_apprx_tanh,
                                    bias=b1_sb.ap()[:, e, fb * NF + fi:fb * NF + fi + 1])
                                first_h = False
                                eng.drain().then_inc(sems['SS'], 1)
                            pub('SS'); ss_gelu[ih_g] = cnt['SS']
                            ih_g += 1
                    ss_gelu_end = cnt['SS']
                    for dcl in range(2):
                        slot2 = w2_ld % 2
                        if me('gpsimd'):
                            if w2_ld >= 2:
                                wait('TS', w2_free[w2_ld - 2])
                            else:
                                wait('TS', TS_p3)
                            eng.dma_start(
                                out=w2b[:, slot2, :, :],
                                in_=w2t.rearrange("e (a p) d -> e p a d", p=128)[
                                    e, :, fb * NF:(fb + 1) * NF,
                                    dcl * 512:(dcl + 1) * 512]
                            ).then_inc(sems['GW2%d' % slot2], 16)
                        pub('GW2%d' % slot2, 16)
                        gd_w2[w2_ld] = ('GW2%d' % slot2, cnt['GW2%d' % slot2])
                        for tc in range(NTC):
                            yb = y_ps.ap()[:, iy_g % 2, :]
                            if me('tensor'):
                                wait(*gd_w2[w2_ld])
                                if tc == 0 and dcl == 0:
                                    wait('SS', ss_gelu_end)
                                    wait('VS', vs_yinit)
                                wait('VS', vs_yevac.get(iy_g - 2, 0))
                                for fi in range(NF):
                                    inst = eng.matmul(
                                        yb, g_sb[:, fi, tc * 128:(tc + 1) * 128],
                                        w2b[:, slot2, fi, :],
                                        start=(fi == 0), stop=(fi == NF - 1))
                                inst.then_inc(sems['TS'], 1)
                            pub('TS'); ts_y = cnt['TS']
                            if tc == NTC - 1:
                                w2_free[w2_ld] = ts_y
                            if tc == NTC - 1 and dcl == 1:
                                ts_y_last = ts_y
                            if me('vector'):
                                wait('TS', ts_y)
                                eng.scalar_tensor_tensor(
                                    y_acc[:, tc, dcl * 512:(dcl + 1) * 512], yb,
                                    w_tm.ap()[:, tcoff + tc, e:e + 1],
                                    y_acc[:, tc, dcl * 512:(dcl + 1) * 512],
                                    ALU.mult, ALU.add)
                                eng.drain().then_inc(sems['VS'], 1)
                            pub('VS'); vs_yevac[iy_g] = cnt['VS']
                            iy_g += 1
                        w2_ld += 1
            vs_y_end = cnt['VS']

            for tc in range(NTC):
                fidx = p * NTC + tc
                sl = fidx % 2
                if me('sync'):
                    if fidx >= 2:
                        wait('VS', vs_fin[fidx - 2])
                    else:
                        wait('VS', VS_p3)
                    wait('SR', 16 * NCHT)
                    eng.dma_start(out=xres_t[sl][:],
                                  in_=xres_hbm[toff + tc * 128: toff + (tc + 1) * 128, :]
                                  ).then_inc(sems['SX%d' % sl], 16)
                pub('SX%d' % sl, 16); sd_f = cnt['SX%d' % sl]
                if me('vector'):
                    wait('SX%d' % sl, sd_f)
                    wait('VS', vs_y_end)
                    if fidx >= 2:
                        wait(*sd_fin[fidx - 2])
                    eng.tensor_tensor(outb[sl][:], y_acc[:, tc, :], xres_t[sl][:], ALU.add)
                    eng.drain().then_inc(sems['VS'], 1)
                pub('VS'); vs_fin[fidx] = cnt['VS']
                if me('sync'):
                    wait('VS', vs_fin[fidx])
                    eng.dma_start(out=out[toff + tc * 128: toff + (tc + 1) * 128, :],
                                  in_=outb[sl][:]).then_inc(sems['SO%d' % sl], 16)
                pub('SO%d' % sl, 16); sd_fin[fidx] = ('SO%d' % sl, cnt['SO%d' % sl])
            vs_pass_done = cnt['VS']
            vs_yend_prev = vs_y_end

        if me('sync'):
            for n in ["SDP", "SR", "SX0", "SX1", "SO0", "SO1"]:
                wait(n, cnt[n])
        if me('gpsimd'):
            for n in ["GDP", "GX0", "GX1", "GW10", "GW11", "GW20", "GW21"]:
                wait(n, cnt[n])
        return cnt

    @block.sync
    def _(e):
        stream('sync', e)

    @block.gpsimd
    def _(e):
        stream('gpsimd', e)

    @block.tensor
    def _(e):
        stream('tensor', e)

    @block.vector
    def _(e):
        stream('vector', e)

    @block.scalar
    def _(e):
        stream('scalar', e)

    blk.__exit__(None, None, None)
    return nc


def make_core_inputs(x_flat, n_cores, T, P, WwT, WrT, gateT, dcol, lacol,
                     w1t_a, w2t_a, b1r, b2_a, seq=None):
    Ttot = x_flat.shape[0]
    if seq is None:
        seq = Ttot // 2 if n_cores > 1 else Ttot
    xT = np.ascontiguousarray(x_flat.T)
    maps = []
    for c in range(n_cores):
        t0 = c * T
        bs = (t0 // seq) * seq
        p_c = t0 - bs
        xTp = np.zeros((D, P + 1), np.float32)
        if p_c:
            xTp[:, P + 1 - p_c:] = xT[:, bs:t0]
        xTm = np.zeros((D, T + 1), np.float32)
        xTm[:, 1:] = xT[:, t0:t0 + T]
        if p_c:
            xTm[:, 0] = xT[:, t0 - 1]
        maps.append(dict(
            xTm=xTm, xTp=xTp, x_tm=np.ascontiguousarray(x_flat[t0:t0 + T]),
            WwT=WwT, WrT=WrT, gateT=gateT, dcol=dcol, lacol=lacol,
            w1t=w1t_a, w2t=w2t_a, b1r=b1r, b2=b2_a))
    return maps


def kernel(L0_attn_out, L0_attn_W_write, L0_attn_W_read, L0_attn_decay,
           L0_attn_log_alpha, L0_ffn_gate_w, L0_ffn_w1, L0_ffn_b1,
           L0_ffn_w2, L0_ffn_b2):
    x = np.ascontiguousarray(np.asarray(L0_attn_out, np.float32).reshape(-1, D))
    WwT = np.ascontiguousarray(np.asarray(L0_attn_W_write, np.float32).T)
    WrT = np.ascontiguousarray(np.asarray(L0_attn_W_read, np.float32).T)
    gT = np.ascontiguousarray(np.asarray(L0_ffn_gate_w, np.float32).T)
    dcol = np.ascontiguousarray(np.broadcast_to(
        np.asarray(L0_attn_decay, np.float32), (128, H)))
    lacol = np.ascontiguousarray(np.broadcast_to(
        np.asarray(L0_attn_log_alpha, np.float32), (128, H)))
    w1 = np.asarray(L0_ffn_w1, np.float32)
    w2 = np.asarray(L0_ffn_w2, np.float32)
    F_ = w1.shape[0]
    w1t_a = np.ascontiguousarray(w1.transpose(1, 2, 0))
    w2t_a = np.ascontiguousarray(w2.transpose(1, 2, 0))
    b1 = np.asarray(L0_ffn_b1, np.float32)
    b1r = np.ascontiguousarray(b1.T.reshape(F_ // 128, 128, E).transpose(1, 2, 0))
    b2_a = np.ascontiguousarray(np.asarray(L0_ffn_b2, np.float32))

    T, P = 2048, 6144
    nc = build_program(T=T, P=P, F=F_)
    maps = make_core_inputs(x, 8, T, P, WwT, WrT, gT, dcol, lacol,
                            w1t_a, w2t_a, b1r, b2_a)
    res = run_bass_kernel_spmd(nc, maps, core_ids=list(range(8)))
    outp = np.concatenate([r["out"] for r in res.results], axis=0)
    return outp.reshape(np.asarray(L0_attn_out).shape).astype(np.float32)
